# version 10
# baseline (speedup 1.0000x reference)
import numpy as np
from contextlib import ExitStack

import concourse.bass as bass
import concourse.mybir as mybir
from concourse.bass_utils import run_bass_kernel_spmd

F32 = mybir.dt.float32
BF16 = mybir.dt.bfloat16
AF = mybir.ActivationFunctionType
ALU = mybir.AluOpType

D = 2048
T = 2048
H = 16
G = 2
HD = 128
HID = 1024
NE = 9
DC = D // 128
HC = HID // 128

_CACHE = {}


class Ctr:
    def __init__(self, nc, stack, name):
        self.sem = stack.enter_context(nc.semaphore(name))
        self.name = name
        self.n = 0

    def inc(self, instr, k=1):
        instr.then_inc(self.sem, k)
        self.n += k
        return self.n


class Eng:
    def __init__(self, raw, name):
        self.raw = raw
        self.name = name
        self.known = {}

    def wait(self, ctr, val):
        if val is None or val <= 0:
            return
        if self.known.get(ctr.name, -1) >= val:
            return
        self.raw.wait_ge(ctr.sem, val)
        self.known[ctr.name] = val


class OpWrap:

    COMPUTE = {
        "tensor_copy", "tensor_tensor", "tensor_add", "tensor_sub",
        "tensor_mul", "tensor_max", "tensor_scalar", "tensor_single_scalar",
        "tensor_scalar_mul", "tensor_scalar_add", "scalar_tensor_tensor",
        "tensor_reduce", "reduce_max", "reduce_sum", "reciprocal", "memset",
        "activation", "copy", "mul", "add", "square", "sqrt", "sign",
        "select", "stream_shuffle", "tensor_relu", "memzero",
    }

    def __init__(self, raw, weng, ctr):
        self._raw = raw
        self._weng = weng
        self._ctr = ctr

    def __getattr__(self, name):
        fn = getattr(self._raw, name)
        if name not in self.COMPUTE:
            return fn

        def wrapped(*a, **k):
            self._weng.wait(self._ctr, self._ctr.n)
            instr = fn(*a, **k)
            self._ctr.inc(instr)
            return instr

        return wrapped


def build_program():
    nc = bass.Bass("TRN2", target_bir_lowering=False)

    xt = nc.dram_tensor("xt", [D, T], F32, kind="ExternalInput").ap()
    wq = nc.dram_tensor("wq", [D, D], F32, kind="ExternalInput").ap()
    wk = nc.dram_tensor("wk", [D, G * HD], F32, kind="ExternalInput").ap()
    wv = nc.dram_tensor("wv", [D, G * HD], F32, kind="ExternalInput").ap()
    wo = nc.dram_tensor("wo", [D, D], F32, kind="ExternalInput").ap()
    bq2 = nc.dram_tensor("bq2", [128, H], F32, kind="ExternalInput").ap()
    bk2 = nc.dram_tensor("bk2", [128, G], F32, kind="ExternalInput").ap()
    bv2 = nc.dram_tensor("bv2", [128, G], F32, kind="ExternalInput").ap()
    bo2 = nc.dram_tensor("bo2", [128, DC], F32, kind="ExternalInput").ap()
    ropet = nc.dram_tensor("ropet", [128, 4 * T], F32, kind="ExternalInput").ap()
    identf = nc.dram_tensor("identf", [128, 128], F32, kind="ExternalInput").ap()
    oneh = nc.dram_tensor("oneh", [8, 8 * 128], F32, kind="ExternalInput").ap()
    gwt = nc.dram_tensor("gwt", [D, 8], F32, kind="ExternalInput").ap()
    w1t = nc.dram_tensor("w1t", [NE, D, HID], F32, kind="ExternalInput").ap()
    w3t = nc.dram_tensor("w3t", [NE, D, HID], F32, kind="ExternalInput").ap()
    w2t = nc.dram_tensor("w2t", [NE, HID, D], F32, kind="ExternalInput").ap()
    yt = nc.dram_tensor("yt", [D, T], F32, kind="ExternalOutput").ap()

    s_qt = nc.dram_tensor("s_qt", [H, 128, T], BF16, kind="Internal").ap()
    s_ao = nc.dram_tensor("s_ao", [D, T], BF16, kind="Internal").ap()
    s_xmid = nc.dram_tensor("s_xmid", [D, T], F32, kind="Internal").ap()
    s_xn2 = nc.dram_tensor("s_xn2", [D, T], BF16, kind="Internal").ap()
    s_w13 = nc.dram_tensor("s_w13", [NE, 2, D, HID], BF16, kind="Internal").ap()
    s_w2c = nc.dram_tensor("s_w2c", [NE, HID, D], BF16, kind="Internal").ap()
    s_cf = nc.dram_tensor("s_cf", [4, 32, 128], BF16, kind="Internal").ap()

    with ExitStack() as top:
        e = top.enter_context

        kv_scope = ExitStack()
        kt = kv_scope.enter_context(
            nc.sbuf_tensor("kt", [128, G * T], BF16, side="right"))
        vtm = kv_scope.enter_context(
            nc.sbuf_tensor("vtm", [128, G * 16 * 128], BF16, side="right"))
        bq2s = e(nc.sbuf_tensor("bq2s", [128, H], F32))
        bk2s = e(nc.sbuf_tensor("bk2s", [128, G], F32))
        bv2s = e(nc.sbuf_tensor("bv2s", [128, G], F32))
        bo2s = e(nc.sbuf_tensor("bo2s", [128, DC], F32))
        onesc = e(nc.sbuf_tensor("onesc", [128, 1], BF16))
        onesr = e(nc.sbuf_tensor("onesr", [1, 128], BF16))
        idf = e(nc.sbuf_tensor("idf", [128, 128], F32))
        gwbf = e(nc.sbuf_tensor("gwbf", [128, DC * 8], BF16))
        epsb = e(nc.sbuf_tensor("epsb", [1, 1], F32))
        rows = e(nc.sbuf_tensor("rows", [1, 2560], F32))
        rowsb = e(nc.sbuf_tensor("rowsb", [1, 1024], BF16))

        pb = [e(nc.psum_tensor(f"pb{i}", [128, 512], F32)) for i in range(8)]

        cP = Ctr(nc, top, "cP")
        cA = Ctr(nc, top, "cA")
        cV = Ctr(nc, top, "cV")
        d_cn = Ctr(nc, top, "d_cn")
        d_wa = Ctr(nc, top, "d_wa")
        d_kv = Ctr(nc, top, "d_kv")
        d_wq = [Ctr(nc, top, f"d_wq{i}") for i in range(4)]
        d_oh = Ctr(nc, top, "d_oh")
        d_wc = Ctr(nc, top, "d_wc")
        d_xf = [Ctr(nc, top, f"d_xf{i}") for i in range(2)]
        d_rp = [Ctr(nc, top, f"d_rp{i}") for i in range(2)]
        d_qs = [Ctr(nc, top, f"d_qs{i}") for i in range(2)]
        d_qh = [Ctr(nc, top, f"d_qh{i}") for i in range(2)]
        d_ao = [Ctr(nc, top, f"d_ao{i}") for i in range(2)]
        d_ac = [Ctr(nc, top, f"d_ac{i}") for i in range(2)]
        d_xc = [Ctr(nc, top, f"d_xc{i}") for i in range(2)]
        d_st = [Ctr(nc, top, f"d_st{i}") for i in range(2)]
        d_q3 = Ctr(nc, top, "d_q3")
        d_cf = Ctr(nc, top, "d_cf")
        d_qe = Ctr(nc, top, "d_qe")
        d_acc = Ctr(nc, top, "d_acc")
        d_yt = Ctr(nc, top, "d_yt")
        d_w1 = Ctr(nc, top, "d_w1")
        d_w3 = Ctr(nc, top, "d_w3")
        d_w2 = [Ctr(nc, top, f"d_w2{i}") for i in range(3)]

        TE = Eng(nc.tensor, "TE")
        SC = Eng(nc.scalar, "SC")
        VE = Eng(nc.vector, "VE")
        SY = Eng(nc.sync, "SY")
        GP = Eng(nc.gpsimd, "GP")
        NV = OpWrap(nc.vector, VE, cV)
        NS = OpWrap(nc.scalar, SC, cA)
        cQ = Ctr(nc, top, "cQ")
        NQ = OpWrap(nc.gpsimd, GP, cQ)

        d_cn.inc(nc.sync.dma_start(bq2s[:], bq2), 16)
        d_cn.inc(nc.sync.dma_start(bk2s[:], bk2), 16)
        d_cn.inc(nc.sync.dma_start(bv2s[:], bv2), 16)
        d_cn.inc(nc.sync.dma_start(bo2s[:], bo2), 16)
        d_cn.inc(nc.sync.dma_start(idf[:], identf), 16)
        d_wa.inc(nc.gpsimd.dma_start(
            gwbf[:].rearrange("p (c e) -> p c e", e=8),
            gwt.rearrange("(c p) e -> p c e", p=128)), 16)
        NV.memset(onesc[:], 1.0)
        NV.memset(onesr[:], 1.0)
        NV.memset(epsb[:], 1e-5)
        v_consts = cV.n

        def load_w_attn(dst, src, out_w):
            return nc.gpsimd.dma_start(
                dst[:].rearrange("p (c o) -> p c o", o=out_w),
                src.rearrange("(c p) o -> p c o", p=128))

        cache_jobs = []
        for ee in range(NE):
            cache_jobs.append((s_w13[ee, 0], w1t[ee]))
            cache_jobs.append((s_w13[ee, 1], w3t[ee]))
            cache_jobs.append((s_w2c[ee], w2t[ee]))
        cache_state = {"i": 0}

        def emit_cache(n):
            for _ in range(n):
                if cache_state["i"] < len(cache_jobs):
                    dst, src = cache_jobs[cache_state["i"]]
                    d_wc.inc(nc.gpsimd.dma_start(dst, src), 16)
                    cache_state["i"] += 1

        all_dma_ctrs = ([d_cn, d_wa, d_kv, d_oh, d_wc, d_qs[0], d_qs[1],
                         d_q3, d_cf, d_qe, d_acc, d_yt, d_w1, d_w3] + d_wq +
                        d_xf + d_rp + d_qh + d_ao + d_ac + d_xc + d_st + d_w2)

        def phase_barrier():
            for eng in (TE, SC, VE, SY, GP):
                eng.wait(cP, cP.n)
                eng.wait(cA, cA.n)
                eng.wait(cV, cV.n)
                eng.wait(cQ, cQ.n)
                for dctr in all_dma_ctrs:
                    eng.wait(dctr, dctr.n)

        ln_state = {"stat_free": [], "bc_free": []}

        def emit_layernorm(xsrc, dst_bf, ct, xbt, sqt, tmp1, ld_wait):
            p_dc = [0] * DC
            for dc in range(DC):
                bbb = dc % len(xbt)
                if ld_wait is not None:
                    VE.wait(ld_wait[0], ld_wait[1])
                if dc >= len(xbt):
                    VE.wait(cP, p_dc[dc - len(xbt)])
                NV.tensor_copy(xbt[bbb][:, 0:ct],
                                             xsrc[:, dc * ct:(dc + 1) * ct])
                NV.tensor_mul(sqt[bbb][:, 0:ct], xbt[bbb][:, 0:ct],
                                            xbt[bbb][:, 0:ct])
                v_sq = cV.n
                TE.wait(cV, v_sq)
                if dc == 0:
                    for wctr, wval in ln_state["stat_free"]:
                        TE.wait(wctr, wval)
                nc.tensor.matmul(pb[6][0:1, 0:ct], onesc[:], xbt[bbb][:, 0:ct],
                                 start=(dc == 0), stop=(dc == DC - 1))
                mm = nc.tensor.matmul(pb[7][0:1, 0:ct], onesc[:], sqt[bbb][:, 0:ct],
                                      start=(dc == 0), stop=(dc == DC - 1))
                cP.inc(mm)
                p_dc[dc] = cP.n
            p_stats = cP.n
            m_sl = rows[0:1, 0:ct]
            m2_sl = rows[0:1, ct:2 * ct]
            t_sl = rows[0:1, 2 * ct:3 * ct]
            v_sl = rows[0:1, 3 * ct:4 * ct]
            r_sl = rows[0:1, 4 * ct:5 * ct]
            VE.wait(cP, p_stats)
            NV.tensor_scalar_mul(m_sl, pb[6][0:1, 0:ct], 1.0 / D)
            NV.tensor_scalar_mul(m2_sl, pb[7][0:1, 0:ct], 1.0 / D)
            NV.tensor_mul(t_sl, m_sl, m_sl)
            NV.tensor_sub(v_sl, m2_sl, t_sl)
            ln_state["stat_free"] = [(cV, cV.n)]
            SC.wait(cV, cV.n)
            NS.activation(t_sl, v_sl, AF.Ln, bias=epsb[0:1, 0:1])
            NS.activation(r_sl, t_sl, AF.Exp, scale=-0.5)
            VE.wait(cA, cA.n)
            NV.tensor_copy(rowsb[0:1, 0:ct], m_sl)
            NV.tensor_copy(rowsb[0:1, ct:2 * ct], r_sl)
            v_mrb = cV.n
            TE.wait(cV, v_mrb)
            for wctr, wval in ln_state["bc_free"]:
                TE.wait(wctr, wval)
            nc.tensor.matmul(pb[4][:, 0:ct], onesr[:], rowsb[0:1, 0:ct],
                             start=True, stop=True)
            mm = nc.tensor.matmul(pb[5][:, 0:ct], onesr[:], rowsb[0:1, ct:2 * ct],
                                  start=True, stop=True)
            cP.inc(mm)
            VE.wait(cP, cP.n)
            for dc in range(DC):
                if tmp1 is None:
                    NV.tensor_sub(xsrc[:, dc * ct:(dc + 1) * ct],
                                  xsrc[:, dc * ct:(dc + 1) * ct], pb[4][:, 0:ct])
                    NV.tensor_mul(dst_bf[:, dc * ct:(dc + 1) * ct],
                                  xsrc[:, dc * ct:(dc + 1) * ct], pb[5][:, 0:ct])
                else:
                    bbb = dc % 2
                    NV.tensor_sub(tmp1[bbb][:, 0:ct],
                                  xsrc[:, dc * ct:(dc + 1) * ct], pb[4][:, 0:ct])
                    NV.tensor_mul(dst_bf[:, dc * ct:(dc + 1) * ct],
                                  tmp1[bbb][:, 0:ct], pb[5][:, 0:ct])
            ln_state["bc_free"] = [(cV, cV.n)]
            return cV.n

        with ExitStack() as ph1:
            e1 = ph1.enter_context
            wqbf = e1(nc.sbuf_tensor("wqbf", [128, DC * D], BF16))
            wkbf = e1(nc.sbuf_tensor("wkbf", [128, DC * G * HD], BF16))
            wvbf = e1(nc.sbuf_tensor("wvbf", [128, DC * G * HD], BF16))
            d_kv.inc(load_w_attn(wkbf, wk, G * HD), 16)
            d_kv.inc(load_w_attn(wvbf, wv, G * HD), 16)
            for qg in range(4):
                d_wq[qg].inc(nc.gpsimd.dma_start(
                    wqbf[:].rearrange("p (c o) -> p c o", o=D)
                    [:, :, qg * 512:(qg + 1) * 512],
                    wq.rearrange("(c p) o -> p c o", p=128)
                    [:, :, qg * 512:(qg + 1) * 512]), 16)
            CT = 512
            xf = e1(nc.sbuf_tensor("xf", [128, DC * CT], F32))
            rp = [e1(nc.sbuf_tensor(f"rp{i}", [128, 4 * CT], F32))
                  for i in range(2)]
            xbt = [e1(nc.sbuf_tensor(f"xbt{i}", [128, CT], BF16)) for i in range(3)]
            sqt = [e1(nc.sbuf_tensor(f"sqt{i}", [128, CT], BF16)) for i in range(3)]
            xn = [e1(nc.sbuf_tensor(f"xn{i}", [128, DC * CT], BF16)) for i in range(2)]
            bcm = e1(nc.sbuf_tensor("bcm", [128, CT], BF16))
            bcr = e1(nc.sbuf_tensor("bcr", [128, CT], BF16))
            qst = [e1(nc.sbuf_tensor(f"qst{i}", [128, CT], BF16)) for i in range(2)]
            qf = [e1(nc.sbuf_tensor(f"qf{i}", [128, CT], F32)) for i in range(2)]
            rot = [e1(nc.sbuf_tensor(f"rot{i}", [128, CT], F32)) for i in range(2)]

            st1 = {
                "rp_free": [0, 0],
                "xf_free_v": 0,
                "xf_half_q": 0,
                "xf_half_v": 0,
                "stat_done": 0,
                "bank_free": [(None, 0)] * 4,
                "evac_free": [(None, 0), (None, 0)],
                "xn_free": [0, 0],
                "qst_free": [0, 0],
                "bc_sb_free": [0, 0],
                "lds": {},
            }
            VE.wait(cV, v_consts)
            SC.wait(d_cn, d_cn.n)

            def p1_loads(tc):
                xpar = tc % 2
                tok0 = tc * CT
                SY.wait(cV, st1["xf_free_v"])
                SY.wait(cQ, st1["xf_half_q"])
                ld0a = d_xf[0].inc(nc.sync.dma_start(
                    xf[:, 0:8 * CT].rearrange("p (c t) -> p c t", t=CT),
                    xt[0:1024, tok0:tok0 + CT]
                    .rearrange("(c p) t -> p c t", p=128)), 16)
                SY.wait(cQ, st1.get("xf_full_q", 0))
                ld0b = d_xf[1].inc(nc.sync.dma_start(
                    xf[:, 8 * CT:16 * CT].rearrange("p (c t) -> p c t", t=CT),
                    xt[1024:2048, tok0:tok0 + CT]
                    .rearrange("(c p) t -> p c t", p=128)), 16)
                SY.wait(cV, st1["rp_free"][xpar])
                ld1 = d_rp[xpar].inc(nc.sync.dma_start(
                    rp[xpar][:].rearrange("p (f t) -> p f t", t=CT),
                    ropet.rearrange("p (f t) -> p f t", t=T)
                    [:, :, tok0:tok0 + CT]), 16)
                st1["lds"][tc] = (ld0a, ld0b, ld1)

            def emit_ln(tc):
                xpar = tc % 2
                ld0a, ld0b, ld1 = st1["lds"][tc]
                p_sq = [0] * DC
                for dc in range(DC):
                    VE.wait(d_xf[0] if dc < 8 else d_xf[1],
                            ld0a if dc < 8 else ld0b)
                    spp = dc % 3
                    if dc >= 3:
                        VE.wait(cP, p_sq[dc - 3])
                    NV.tensor_copy(xbt[spp][:], xf[:, dc * CT:(dc + 1) * CT])
                    NV.tensor_mul(sqt[spp][:], xbt[spp][:], xbt[spp][:])
                    v_sq = cV.n
                    TE.wait(cV, v_sq)
                    if dc == 0:
                        for wctr, wval in ln_state["stat_free"]:
                            TE.wait(wctr, wval)
                    nc.tensor.matmul(pb[6][0:1, 0:CT], onesc[:], xbt[spp][:],
                                     start=(dc == 0), stop=(dc == DC - 1))
                    mm = nc.tensor.matmul(pb[7][0:1, 0:CT], onesc[:], sqt[spp][:],
                                          start=(dc == 0), stop=(dc == DC - 1))
                    cP.inc(mm)
                    p_sq[dc] = cP.n
                st1["stat_done"] = cP.n
                st1["xf_free_v"] = cV.n
                ct = CT
                m_sl = rows[0:1, 0:ct]
                m2_sl = rows[0:1, ct:2 * ct]
                t_sl = rows[0:1, 2 * ct:3 * ct]
                v_sl = rows[0:1, 3 * ct:4 * ct]
                r_sl = rows[0:1, 4 * ct:5 * ct]
                VE.wait(cP, st1["stat_done"])
                NV.tensor_scalar_mul(m_sl, pb[6][0:1, 0:ct], 1.0 / D)
                NV.tensor_scalar_mul(m2_sl, pb[7][0:1, 0:ct], 1.0 / D)
                NV.tensor_mul(t_sl, m_sl, m_sl)
                NV.tensor_sub(v_sl, m2_sl, t_sl)
                ln_state["stat_free"] = [(cV, cV.n)]
                SC.wait(cV, cV.n)
                NS.activation(t_sl, v_sl, AF.Ln, bias=epsb[0:1, 0:1])
                NS.activation(r_sl, t_sl, AF.Exp, scale=-0.5)
                VE.wait(cA, cA.n)
                NV.tensor_copy(rowsb[0:1, 0:ct], m_sl)
                NV.tensor_copy(rowsb[0:1, ct:2 * ct], r_sl)
                v_mrb = cV.n
                TE.wait(cV, v_mrb)
                for wctr, wval in ln_state["bc_free"]:
                    TE.wait(wctr, wval)
                nc.tensor.matmul(pb[4][:, 0:ct], onesr[:], rowsb[0:1, 0:ct],
                                 start=True, stop=True)
                mm = nc.tensor.matmul(pb[5][:, 0:ct], onesr[:],
                                      rowsb[0:1, ct:2 * ct],
                                      start=True, stop=True)
                cP.inc(mm)
                p_bc = cP.n
                SC.wait(cP, p_bc)
                SC.wait(cQ, st1["bc_sb_free"][0])
                NS.copy(bcm[:], pb[4][:, 0:ct])
                NS.copy(bcr[:], pb[5][:, 0:ct])
                a_bc = cA.n
                GP.wait(cA, a_bc)
                GP.wait(cP, st1["xn_free"][xpar])
                q_half = 0
                for dc in range(DC):
                    GP.wait(d_xf[0] if dc < 8 else d_xf[1],
                            ld0a if dc < 8 else ld0b)
                    NQ.tensor_sub(xf[:, dc * CT:(dc + 1) * CT],
                                  xf[:, dc * CT:(dc + 1) * CT], bcm[:])
                    NQ.tensor_mul(xn[xpar][:, dc * CT:(dc + 1) * CT],
                                  xf[:, dc * CT:(dc + 1) * CT], bcr[:])
                    if dc == 7:
                        q_half = cQ.n
                st1["xf_half_v"] = 0
                st1["xf_half_q"] = q_half
                st1["xf_full_q"] = cQ.n
                st1["bc_sb_free"] = (cQ.n, 0)
                ln_state["bc_free"] = [(cQ, cQ.n)]
                st1["q_xn", tc] = (0, cQ.n)

            def emit_qkv_chain(tc, oi, kind, idx):
                xpar = tc % 2
                tok0 = tc * CT
                bank = oi % 3
                vq, qq_ = st1["q_xn", tc]
                TE.wait(cV, vq)
                TE.wait(cQ, qq_)
                if kind == "q":
                    TE.wait(d_wq[idx // 4], 16)
                else:
                    TE.wait(d_kv, d_kv.n)
                wctr, wval = st1["bank_free"][bank]
                if wctr is not None:
                    TE.wait(wctr, wval)
                for dc in range(DC):
                    if kind == "q":
                        lhs = wqbf[:, dc * D + idx * 128: dc * D + (idx + 1) * 128]
                    elif kind == "k":
                        lhs = wkbf[:, dc * G * HD + idx * 128:
                                   dc * G * HD + (idx + 1) * 128]
                    else:
                        lhs = wvbf[:, dc * G * HD + idx * 128:
                                   dc * G * HD + (idx + 1) * 128]
                    mm = nc.tensor.matmul(pb[bank][:, 0:CT], lhs,
                                          xn[xpar][:, dc * CT:(dc + 1) * CT],
                                          start=(dc == 0), stop=(dc == DC - 1))
                cP.inc(mm)
                p_oc = cP.n
                if oi == 19:
                    st1["xn_free"][xpar] = p_oc
                r = oi % 2
                SC.wait(cP, p_oc)
                pctr, pval = st1["evac_free"][r]
                if pctr is not None:
                    SC.wait(pctr, pval)
                if kind == "q":
                    bias = bq2s[:, idx:idx + 1]
                elif kind == "k":
                    bias = bk2s[:, idx:idx + 1]
                else:
                    bias = bv2s[:, idx:idx + 1]
                NS.activation(qf[r][:], pb[bank][:, 0:CT], AF.Identity, bias=bias)
                st1["bank_free"][bank] = (cA, cA.n)
                a_evac = cA.n
                if kind in ("q", "k"):
                    NS.copy(rot[r][0:64, :], qf[r][64:128, :])
                    NS.copy(rot[r][64:128, :], qf[r][0:64, :])
                    a_rot = cA.n
                    co = 0 if kind == "q" else 2 * CT
                    si = CT if kind == "q" else 3 * CT
                    xpar2 = tc % 2
                    VE.wait(cA, a_rot)
                    VE.wait(d_rp[xpar2], st1["lds"][tc][2])
                    NV.tensor_mul(qf[r][:], qf[r][:], rp[xpar2][:, co:co + CT])
                    NV.tensor_mul(rot[r][:], rot[r][:], rp[xpar2][:, si:si + CT])
                    if kind == "q":
                        qr2 = r
                        VE.wait(d_qs[qr2], st1["qst_free"][qr2])
                        NV.tensor_add(qst[qr2][:], qf[r][:], rot[r][:])
                        v_q = cV.n
                        SY.wait(cV, v_q)
                        st1["qst_free"][qr2] = d_qs[qr2].inc(nc.sync.dma_start(
                            s_qt[idx, :, tok0:tok0 + CT], qst[qr2][:]), 16)
                        st1["evac_free"][r] = (cV, cV.n)
                    else:
                        dst = kt[:, idx * T + tok0: idx * T + tok0 + CT]
                        NV.tensor_add(dst, qf[r][:], rot[r][:])
                        st1["evac_free"][r] = (cV, cV.n)
                else:
                    for half in range(4):
                        kc = tc * 4 + half
                        TE.wait(cA, a_evac)
                        TE.wait(d_cn, d_cn.n)
                        wctr2, wval2 = st1["bank_free"][3]
                        if wctr2 is not None:
                            TE.wait(wctr2, wval2)
                        cP.inc(nc.tensor.transpose(
                            pb[3][0:128, 0:128],
                            qf[r][:, half * 128:(half + 1) * 128], idf[:]))
                        p_tr = cP.n
                        VE.wait(cP, p_tr)
                        col = (idx * 16 + kc) * 128
                        NV.tensor_copy(vtm[:, col:col + 128],
                                       pb[3][0:128, 0:128])
                        st1["bank_free"][3] = (cV, cV.n)
                    st1["evac_free"][r] = (cV, cV.n)
                if oi == 19:
                    st1["rp_free"][tc % 2] = cV.n

            oc_list = [("k", 0), ("k", 1), ("v", 0), ("v", 1)] + \
                      [("q", hh) for hh in range(H)]
            p1_loads(0)
            emit_ln(0)
            for tc in range(4):
                if tc + 1 < 4:
                    p1_loads(tc + 1)
                for oi in range(8):
                    emit_qkv_chain(tc, oi, *oc_list[oi])
                if tc + 1 < 4:
                    emit_ln(tc + 1)
                for oi in range(8, 20):
                    emit_qkv_chain(tc, oi, *oc_list[oi])
                emit_cache(2)

        p1_vend = cV.n
        p1_pend = cP.n
        phase_barrier()
        wo_scope = ExitStack()
        wobf = wo_scope.enter_context(nc.sbuf_tensor("wobf", [128, DC * D], BF16))
        with ExitStack() as ph2a:
            e2 = ph2a.enter_context
            qh = [e2(nc.sbuf_tensor(f"qh{i}", [128, T], BF16)) for i in range(2)]
            et = e2(nc.sbuf_tensor("et", [128, 16 * 512], BF16))
            ao = [e2(nc.sbuf_tensor(f"ao{i}", [128, 512], BF16)) for i in range(2)]
            rsb = e2(nc.sbuf_tensor("rsb", [128, 512], F32))

            GP.wait(cP, p1_pend)
            d_wo = d_wa.inc(load_w_attn(wobf, wo, D), 16)

            qh_free = [0, 0]
            qh_ld = [0, 0]
            sc_bank_free = [0, 0, 0, 0]
            od_free = [[0, 0], [0, 0]]
            et_free = [0] * 16
            ao_free = [0, 0]

            TE.wait(cV, p1_vend)
            SY.wait(d_qs[0], d_qs[0].n)
            SY.wait(d_qs[1], d_qs[1].n)
            qh_ld[0] = d_qh[0].inc(nc.sync.dma_start(qh[0][:], s_qt[0]), 16)

            for h in range(H):
                b = h % 2
                g = h // (H // G)
                if h + 1 < H:
                    SY.wait(cP, qh_free[(h + 1) % 2])
                    qh_ld[(h + 1) % 2] = d_qh[(h + 1) % 2].inc(
                        nc.sync.dma_start(qh[(h + 1) % 2][:], s_qt[h + 1]), 16)
                for qc in range(4):
                    emit_cache(1)
                    q0 = qc * 512
                    qidx = h * 4 + qc
                    par2 = qidx % 2
                    ob, db = 4 + 2 * par2, 5 + 2 * par2
                    p_s = [0] * 16
                    a_e = [0] * 16
                    TE.wait(d_qh[b], qh_ld[b])

                    def emit_score(kc):
                        bank = kc % 4
                        TE.wait(cA, sc_bank_free[bank])
                        cP.inc(nc.tensor.matmul(
                            pb[bank][:, 0:512],
                            kt[:, g * T + kc * 128: g * T + (kc + 1) * 128],
                            qh[b][:, q0:q0 + 512], start=True, stop=True))
                        p_s[kc] = cP.n

                    for kc in range(4):
                        emit_score(kc)
                    for kc in range(16):
                        bank = kc % 4
                        SC.wait(cP, p_s[kc])
                        SC.wait(cP, et_free[kc])
                        NS.activation(
                            et[:, kc * 512:(kc + 1) * 512], pb[bank][:, 0:512],
                            AF.Exp)
                        a_e[kc] = cA.n
                        sc_bank_free[bank] = a_e[kc]
                        if kc + 4 < 16:
                            emit_score(kc + 4)
                        TE.wait(cA, a_e[kc])
                        if kc == 0:
                            TE.wait(cV, od_free[par2][0])
                            TE.wait(cV, od_free[par2][1])
                        col = (g * 16 + kc) * 128
                        nc.tensor.matmul(pb[ob][:, 0:512], vtm[:, col:col + 128],
                                         et[:, kc * 512:(kc + 1) * 512],
                                         start=(kc == 0), stop=(kc == 15))
                        mm = nc.tensor.matmul(pb[db][0:1, 0:512], onesc[:],
                                              et[:, kc * 512:(kc + 1) * 512],
                                              start=(kc == 0), stop=(kc == 15))
                    cP.inc(mm)
                    p_avd = cP.n
                    for kc in range(16):
                        et_free[kc] = p_avd
                    if qc == 3:
                        qh_free[b] = p_avd
                    SC.wait(cP, p_avd)
                    NS.activation(rows[0:1, 0:512], pb[db][0:1, 0:512], AF.Ln)
                    NS.activation(rows[0:1, 512:1024], rows[0:1, 0:512],
                                  AF.Exp, scale=-1.0)
                    a_rec = cA.n
                    VE.wait(cA, a_rec)
                    NV.tensor_copy(rowsb[0:1, 0:512], rows[0:1, 512:1024])
                    v_rec = cV.n
                    TE.wait(cV, v_rec)
                    TE.wait(cA, a_rec)
                    cP.inc(nc.tensor.matmul(pb[db][:, 0:512], onesr[:],
                                            rowsb[0:1, 0:512], start=True, stop=True))
                    p_r = cP.n
                    VE.wait(cP, p_r)
                    NV.tensor_copy(rsb[:], pb[db][:, 0:512])
                    od_free[par2][1] = cV.n
                    b2 = qidx % 2
                    VE.wait(d_ao[b2], ao_free[b2])
                    NV.tensor_mul(ao[b2][:], pb[ob][:, 0:512], rsb[:])
                    od_free[par2][0] = cV.n
                    v_ao = cV.n
                    SY.wait(cV, v_ao)
                    ao_free[b2] = d_ao[b2].inc(nc.sync.dma_start(
                        s_ao[h * 128:(h + 1) * 128, q0:q0 + 512], ao[b2][:]), 16)
            emit_cache(99)
            ao_store_all = (d_ao[0].n, d_ao[1].n)
        kv_scope.close()

        phase_barrier()
        with ExitStack() as ph2b:
            e3 = ph2b.enter_context
            CT = 512
            aoc = [e3(nc.sbuf_tensor("aoc0", [128, DC * CT], BF16))]
            xfc = [e3(nc.sbuf_tensor(f"xfc{i}", [128, DC * CT], F32)) for i in range(2)]
            xbt = [e3(nc.sbuf_tensor("xbt2_0", [128, CT], BF16))]
            sqt = [e3(nc.sbuf_tensor("sqt2_0", [128, CT], BF16))]
            xn2b = [e3(nc.sbuf_tensor(f"xn2b{i}", [128, DC * CT], BF16))
                    for i in range(2)]

            sctm = e3(nc.sbuf_tensor("sctm", [128, 32], F32))
            sgt = sctm
            rank = e3(nc.sbuf_tensor("rank", [128, 32], F32))
            maskt = e3(nc.sbuf_tensor("maskt", [128, 32], F32))
            cbf32 = [e3(nc.sbuf_tensor(f"cbf32_{i}", [128, 32], F32,
                                        side="right"))
                     for i in range(2)]
            cbsb = e3(nc.sbuf_tensor("cbsb", [32, 128], BF16, side="right"))
            gate_state = {"cb_free": 0, "sctm_free": 0, "g_bank_a": 0,
                          "v_c": [0, 0], "p_g_last": 0}

            aoc_free = [0]
            xfc_free = [0, 0]
            wo_bank_free = [0, 0, 0]
            xn2_free = 0
            v_xmid = [0] * 4
            ln_state["stat_free"] = [(cV, cV.n)]
            ln_state["bc_free"] = [(cV, cV.n)]
            SY.wait(d_ao[0], ao_store_all[0])
            SY.wait(d_ao[1], ao_store_all[1])

            def gate_logits(qq):
                gpar = qq % 2
                gq = qq % 2
                for tt in range(4):
                    TE.wait(cV, v_xn2_of[qq])
                    TE.wait(d_wa, d_wa.n)
                    for wctr, wval in ln_state["stat_free"]:
                        TE.wait(wctr, wval)
                    TE.wait(cA, gate_state["g_bank_a"])
                    for dc in range(DC):
                        mm = nc.tensor.matmul(
                            pb[6][:, 0:8],
                            xn2b[gq][:, dc * CT + tt * 128:
                                     dc * CT + (tt + 1) * 128],
                            gwbf[:, dc * 8:(dc + 1) * 8],
                            start=(dc == 0), stop=(dc == DC - 1))
                    cP.inc(mm)
                    p_g = cP.n
                    gate_state["p_g_last"] = p_g
                    SC.wait(cP, p_g)
                    SC.wait(cV, gate_state["sctm_free"])
                    NS.activation(sgt[:, tt * 8:(tt + 1) * 8],
                                  pb[6][:, 0:8], AF.Exp, scale=-1.0)
                    gate_state["g_bank_a"] = cA.n
                a_sg = cA.n
                ln_state["stat_free"] = (ln_state["stat_free"] +
                                         [(cA, gate_state["g_bank_a"])])
                VE.wait(cA, a_sg)
                NV.tensor_scalar_add(maskt[:], sgt[:], 1.0)
                NV.reciprocal(sctm[:], maskt[:])
                NV.memset(rank[:], 0.0)
                for j in range(8):
                    for tt in range(4):
                        if j < 7:
                            NV.scalar_tensor_tensor(
                                rank[:, tt * 8 + j + 1: tt * 8 + 8],
                                sctm[:, tt * 8 + j + 1: tt * 8 + 8],
                                sctm[:, tt * 8 + j: tt * 8 + j + 1],
                                rank[:, tt * 8 + j + 1: tt * 8 + 8],
                                ALU.is_le, ALU.add)
                        if j > 0:
                            NV.scalar_tensor_tensor(
                                rank[:, tt * 8: tt * 8 + j],
                                sctm[:, tt * 8: tt * 8 + j],
                                sctm[:, tt * 8 + j: tt * 8 + j + 1],
                                rank[:, tt * 8: tt * 8 + j],
                                ALU.is_lt, ALU.add)
                NV.tensor_single_scalar(maskt[:], rank[:], 2.0, ALU.is_lt)
                NV.tensor_mul(cbf32[gpar][:], maskt[:], sctm[:])
                gate_state["sctm_free"] = cV.n
                gate_state["v_c"][gpar] = cV.n

            def gate_finish(qq):
                gpar = qq % 2
                TE.wait(cV, gate_state["v_c"][gpar])
                for wctr, wval in ln_state["stat_free"]:
                    TE.wait(wctr, wval)
                cP.inc(nc.tensor.transpose(pb[7][0:32, 0:128],
                                           cbf32[gpar][:], idf[:]))
                p_t = cP.n
                VE.wait(cP, p_t)
                VE.wait(d_cf, gate_state["cb_free"])
                NV.tensor_copy(cbsb[:], pb[7][0:32, 0:128])
                v_cb = cV.n
                ln_state["stat_free"] = ln_state["stat_free"] + [(cV, v_cb)]
                SY.wait(cV, v_cb)
                cb_store = d_cf.inc(nc.sync.dma_start(s_cf[qq], cbsb[:]), 16)
                gate_state["cb_free"] = cb_store

            v_xn2_of = [0] * 4

            def p2b_load_ao(tc):
                tok0 = tc * CT
                SY.wait(cP, aoc_free[0])
                return d_ac[0].inc(nc.sync.dma_start(
                    aoc[0][:].rearrange("p (c t) -> p c t", t=CT),
                    s_ao[:, tok0:tok0 + CT]
                    .rearrange("(c p) t -> p c t", p=128)), 16)

            def p2b_load_x(tc):
                b = tc % 2
                tok0 = tc * CT
                SY.wait(d_st[b], xfc_free[b])
                return d_xc[b].inc(nc.sync.dma_start(
                    xfc[b][:].rearrange("p (c t) -> p c t", t=CT),
                    xt[:, tok0:tok0 + CT]
                    .rearrange("(c p) t -> p c t", p=128)), 16)

            lda_next = p2b_load_ao(0)
            ldx_next = p2b_load_x(0)
            for tc in range(7):
                pass_dummy = 0
                b = tc % 2
                if tc < 4:
                    tok0 = tc * CT
                    lda, ldx = lda_next, ldx_next
                    for oc in range(DC):
                        bank = oc % 3
                        TE.wait(d_ac[0], lda)
                        TE.wait(d_wa, d_wo)
                        TE.wait(cV, wo_bank_free[bank])
                        for dc in range(DC):
                            mm = nc.tensor.matmul(
                                pb[bank][:, 0:CT],
                                wobf[:, dc * D + oc * 128: dc * D + (oc + 1) * 128],
                                aoc[0][:, dc * CT:(dc + 1) * CT],
                                start=(dc == 0), stop=(dc == DC - 1))
                        cP.inc(mm)
                        p_oc = cP.n
                        if oc == DC - 1:
                            aoc_free[0] = p_oc
                        VE.wait(cP, p_oc)
                        VE.wait(d_xc[b], ldx)
                        NV.scalar_tensor_tensor(
                            xfc[b][:, oc * CT:(oc + 1) * CT],
                            pb[bank][:, 0:CT], bo2s[:, oc:oc + 1],
                            xfc[b][:, oc * CT:(oc + 1) * CT],
                            ALU.add, ALU.add)
                        wo_bank_free[bank] = cV.n
                    v_xmid[tc] = cV.n
                    if tc + 1 < 4:
                        lda_next = p2b_load_ao(tc + 1)
                    SY.wait(cV, v_xmid[tc])
                    d_st[b].inc(nc.sync.dma_start(
                        s_xmid[:, tok0:tok0 + CT]
                        .rearrange("(c p) t -> p c t", p=128),
                        xfc[b][:].rearrange("p (c t) -> p c t", t=CT)), 16)
                if 1 <= tc <= 4:
                    pb_ = (tc - 1) % 2
                    ptok = (tc - 1) * CT
                    if tc > 1:
                        VE.wait(d_st[tc % 2], xn2_free)
                    VE.wait(cV, v_xmid[tc - 1])
                    VE.wait(d_st[pb_], d_st[pb_].n)
                    VE.wait(cP, gate_state["p_g_last"])
                    v_xn2 = emit_layernorm(xfc[pb_], xn2b[pb_], CT,
                                           xbt, sqt, None, None)
                    v_xn2_of[tc - 1] = v_xn2
                    SY.wait(cV, v_xn2)
                    st2 = d_st[pb_].inc(nc.sync.dma_start(
                        s_xn2[:, ptok:ptok + CT]
                        .rearrange("(c p) t -> p c t", p=128),
                        xn2b[pb_][:].rearrange("p (c t) -> p c t", t=CT)), 16)
                    xn2_free = st2
                    xfc_free[pb_] = st2
                if 2 <= tc <= 5:
                    gate_logits(tc - 2)
                if tc + 1 < 4:
                    ldx_next = p2b_load_x(tc + 1)
                if 3 <= tc <= 6:
                    gate_finish(tc - 3)
        cft_ld = d_cf.n
        wo_scope.close()

        phase_barrier()
        with ExitStack() as ph3e:
            e5 = ph3e.enter_context
            ohbf = e5(nc.sbuf_tensor("ohbf", [8, 8 * 128], BF16))
            cft = e5(nc.sbuf_tensor("cft", [8, T], BF16))
            SY.wait(d_cf, d_cf.n)
            for qq4 in range(4):
                d_cf.inc(nc.sync.dma_start(
                    cft[:, qq4 * 512:(qq4 + 1) * 512]
                    .rearrange("e (tt t) -> e tt t", t=128),
                    s_cf[qq4].rearrange("(tt e) t -> e tt t", e=8)), 16)
            cft_ld = d_cf.n
            d_oh.inc(nc.gpsimd.dma_start(ohbf[:], oneh), 16)
            xn2q = e5(nc.sbuf_tensor("xn2q", [128, DC * 512], BF16))
            acc = e5(nc.sbuf_tensor("acc", [128, DC * 512], F32))
            xe = [e5(nc.sbuf_tensor(f"xe{i}", [128, DC * 512], BF16)) for i in range(2)]
            ball = e5(nc.sbuf_tensor("ball", [128, 8 * 512], BF16))
            ubf = e5(nc.sbuf_tensor("ubf", [128, HC * 512], BF16))
            hbf = ubf
            w1b = e5(nc.sbuf_tensor("w1b", [128, DC * HID], BF16))
            w3b = e5(nc.sbuf_tensor("w3b", [128, DC * HID], BF16))
            w2h = [e5(nc.sbuf_tensor(f"w2h{i}", [128, 4 * 1024], BF16))
                   for i in range(2)]

            bank_free = [("V", 0)] * 8
            xn2q_free = 0
            acc_free = 0
            xe_free = [0, 0]
            ball_free = 0
            ubf_free = 0
            hbf_free = 0
            w1_free = 0
            w3_free = 0
            w2h_free = [0, 0]
            w2h_ld = [0, 0]
            w1_ld = 0
            w3_ld = 0
            xe_pre = {}

            def wait_bank(engine, k):
                who, val = bank_free[k]
                engine.wait(cV if who == "V" else cA, val)

            eorder = [8, 0, 1, 2, 3, 4, 5, 6, 7]
            pairs = [(qq, ee) for qq in range(4) for ee in eorder]

            GP.wait(d_wc, d_wc.n)
            w1_ld = d_w1.inc(nc.gpsimd.dma_start(
                w1b[:].rearrange("p (c o) -> p c o", o=HID),
                s_w13[8, 0].rearrange("(c p) o -> p c o", p=128)), 16)
            w3_ld = d_w3.inc(nc.gpsimd.dma_start(
                w3b[:].rearrange("p (c o) -> p c o", o=HID),
                s_w13[8, 1].rearrange("(c p) o -> p c o", p=128)), 16)

            for pi, (qq, ee) in enumerate(pairs):
                t0 = qq * 512
                first_of_quarter = (ee == eorder[0])
                if first_of_quarter:
                    SY.wait(cP, xn2q_free)
                    SY.wait(d_cf, cft_ld)
                    ldq = d_qe.inc(nc.sync.dma_start(
                        xn2q[:].rearrange("p (c t) -> p c t", t=512),
                        s_xn2[:, t0:t0 + 512].rearrange("(c p) t -> p c t", p=128)), 16)
                    SY.wait(d_yt, acc_free)
                    lda = d_acc.inc(nc.sync.dma_start(
                        acc[:].rearrange("p (c t) -> p c t", t=512),
                        s_xmid[:, t0:t0 + 512].rearrange("(c p) t -> p c t", p=128)), 16)
                    SC.wait(cV, ball_free)
                    for ex in range(8):
                        TE.wait(d_cf, cft_ld)
                        TE.wait(d_oh, d_oh.n)
                        wait_bank(TE, 6)
                        cP.inc(nc.tensor.matmul(
                            pb[6][:, 0:512], ohbf[0:8, ex * 128:(ex + 1) * 128],
                            cft[0:8, t0:t0 + 512], start=True, stop=True))
                        p_b = cP.n
                        SC.wait(cP, p_b)
                        NS.copy(ball[:, ex * 512:(ex + 1) * 512],
                                              pb[6][:, 0:512])
                        bank_free[6] = ("A", cA.n)
                    a_ball = cA.n
                    ld_quarter = (ldq, lda)

                ldq, lda = ld_quarter
                par = pi % 2
                routed = ee != 8
                if routed:
                    v_xe = xe_pre.pop(pi)
                    rhs_buf = xe[par]
                else:
                    v_xe = 0
                    rhs_buf = xn2q

                def w2_load(ci):
                    p2c, hhc = divmod(ci, 2)
                    buf = ci % 2
                    GP.wait(cP, w2h_free[buf])
                    w2h_ld[buf] = d_w2[buf].inc(nc.gpsimd.dma_start(
                        w2h[buf][:].rearrange("p (c o) -> p c o", o=1024),
                        s_w2c[ee][hhc * 512:(hhc + 1) * 512,
                                  p2c * 1024:(p2c + 1) * 1024]
                        .rearrange("(c p) o -> p c o", p=128)), 16)
                w2_load(0)
                w2_load(1)

                TE.wait(d_w1, w1_ld)
                TE.wait(cV, v_xe)
                if not routed:
                    TE.wait(d_qe, ldq)
                for dc in range(DC):
                    for oc in range(HC):
                        if dc == 0:
                            wait_bank(TE, oc)
                        mm = nc.tensor.matmul(
                            pb[oc][:, 0:512],
                            w1b[:, dc * HID + oc * 128: dc * HID + (oc + 1) * 128],
                            rhs_buf[:, dc * 512:(dc + 1) * 512],
                            start=(dc == 0), stop=(dc == DC - 1))
                        if dc == DC - 1:
                            cP.inc(mm)
                p_a1 = cP.n
                w1_free = p_a1
                if pi + 1 < len(pairs):
                    nqq, nee = pairs[pi + 1]
                    GP.wait(cP, w1_free)
                    w1_ld_next = d_w1.inc(nc.gpsimd.dma_start(
                        w1b[:].rearrange("p (c o) -> p c o", o=HID),
                        s_w13[nee, 0].rearrange("(c p) o -> p c o", p=128)), 16)
                a_u = [0] * HC
                for oc in range(HC):
                    SC.wait(cP, p_a1 - 7 + oc)
                    SC.wait(cP, hbf_free)
                    NS.activation(ubf[:, oc * 512:(oc + 1) * 512],
                                                pb[oc][:, 0:512], AF.Silu)
                    a_u[oc] = cA.n
                    bank_free[oc] = ("A", cA.n)
                TE.wait(d_w3, w3_ld)
                for oc in range(HC):
                    TE.wait(cA, a_u[oc])
                    for dc in range(DC):
                        mm = nc.tensor.matmul(
                            pb[oc][:, 0:512],
                            w3b[:, dc * HID + oc * 128: dc * HID + (oc + 1) * 128],
                            rhs_buf[:, dc * 512:(dc + 1) * 512],
                            start=(dc == 0), stop=(dc == DC - 1))
                    cP.inc(mm)
                p_a3 = cP.n
                w3_free = p_a3
                xe_free[par] = p_a3
                if not routed:
                    pass
                if pi + 1 < len(pairs):
                    GP.wait(cP, w3_free)
                    w3_ld_next = d_w3.inc(nc.gpsimd.dma_start(
                        w3b[:].rearrange("p (c o) -> p c o", o=HID),
                        s_w13[nee, 1].rearrange("(c p) o -> p c o", p=128)), 16)
                v_h = [0] * HC
                for oc in range(HC):
                    VE.wait(cP, p_a3 - 7 + oc)
                    VE.wait(cP, hbf_free)
                    NV.tensor_mul(hbf[:, oc * 512:(oc + 1) * 512],
                                              ubf[:, oc * 512:(oc + 1) * 512],
                                              pb[oc][:, 0:512])
                    v_h[oc] = cV.n
                    bank_free[oc] = ("V", cV.n)
                ubf_free = cV.n
                if pi + 1 < len(pairs):
                    nqq2, nee2 = pairs[pi + 1]
                    if nee2 != 8 and nqq2 == qq:
                        npar = (pi + 1) % 2
                        VE.wait(d_qe, ldq)
                        VE.wait(cA, a_ball)
                        VE.wait(cP, xe_free[npar])
                        for dc in range(DC):
                            NV.tensor_mul(
                                xe[npar][:, dc * 512:(dc + 1) * 512],
                                xn2q[:, dc * 512:(dc + 1) * 512],
                                ball[:, nee2 * 512:(nee2 + 1) * 512])
                        xe_pre[pi + 1] = cV.n
                p_w2 = {}
                for p2 in range(2):
                    for hh in range(2):
                        ci = p2 * 2 + hh
                        buf = ci % 2
                        if ci >= 2:
                            w2_load(ci)
                        TE.wait(d_w2[buf], w2h_ld[buf])
                        for hc4 in range(4):
                            hc = hh * 4 + hc4
                            TE.wait(cV, v_h[hc])
                            for j in range(8):
                                if hc == 0:
                                    wait_bank(TE, j)
                                mm = nc.tensor.matmul(
                                    pb[j][:, 0:512],
                                    w2h[buf][:, hc4 * 1024 + j * 128:
                                             hc4 * 1024 + (j + 1) * 128],
                                    hbf[:, hc * 512:(hc + 1) * 512],
                                    start=(hc == 0), stop=(hc == HC - 1))
                                if hc == HC - 1:
                                    cP.inc(mm)
                                    p_w2[p2 * 8 + j] = cP.n
                        if hh == 0:
                            cP.inc(mm)
                        w2h_free[buf] = cP.n
                    for j in range(8):
                        oc2 = p2 * 8 + j
                        VE.wait(cP, p_w2[oc2])
                        VE.wait(d_acc, lda)
                        NV.scalar_tensor_tensor(
                            acc[:, oc2 * 512:(oc2 + 1) * 512],
                            pb[j][:, 0:512], 1.0,
                            acc[:, oc2 * 512:(oc2 + 1) * 512],
                            ALU.mult, ALU.add)
                        bank_free[j] = ("V", cV.n)
                hbf_free = cP.n
                v_acc_done = cV.n
                if routed and ee == 7:
                    SY.wait(cV, v_acc_done)
                    acc_free = d_yt.inc(nc.sync.dma_start(
                        yt[:, t0:t0 + 512].rearrange("(c p) t -> p c t", p=128),
                        acc[:].rearrange("p (c t) -> p c t", t=512)), 16)
                    xn2q_free = cP.n
                    ball_free = cV.n
                if pi + 1 < len(pairs):
                    w1_ld = w1_ld_next
                    w3_ld = w3_ld_next

            SY.wait(d_yt, d_yt.n)
    return nc


def _host_prep(inputs):
    x = np.asarray(inputs["L0_attn_x"], np.float32)
    wq = np.ascontiguousarray(np.asarray(inputs["L0_attn_wq"], np.float32))
    wk = np.ascontiguousarray(np.asarray(inputs["L0_attn_wk"], np.float32))
    wv = np.ascontiguousarray(np.asarray(inputs["L0_attn_wv"], np.float32))
    wo = np.ascontiguousarray(np.asarray(inputs["L0_attn_wo"], np.float32))
    bq = np.asarray(inputs["L0_attn_bq"], np.float32)
    bk = np.asarray(inputs["L0_attn_bk"], np.float32)
    bv = np.asarray(inputs["L0_attn_bv"], np.float32)
    bo = np.asarray(inputs["L0_attn_bo"], np.float32)
    gate_w = np.asarray(inputs["L0_ffn_gate_w"], np.float32)
    w1 = np.asarray(inputs["L0_ffn_w1"], np.float32)
    w2 = np.asarray(inputs["L0_ffn_w2"], np.float32)
    w3 = np.asarray(inputs["L0_ffn_w3"], np.float32)
    sw1 = np.asarray(inputs["L0_ffn_sw1"], np.float32)
    sw2 = np.asarray(inputs["L0_ffn_sw2"], np.float32)
    sw3 = np.asarray(inputs["L0_ffn_sw3"], np.float32)

    bq2 = np.ascontiguousarray(bq.reshape(H, 128).T)
    bk2 = np.ascontiguousarray(bk.reshape(G, 128).T)
    bv2 = np.ascontiguousarray(bv.reshape(G, 128).T)
    bo2 = np.ascontiguousarray(bo.reshape(DC, 128).T)

    inv_freq = (1.0 / (np.float32(10000.0) **
                       (np.arange(0, HD, 2, dtype=np.float32) / np.float32(HD))))
    t = np.arange(T, dtype=np.float32)
    freqs = np.concatenate([np.outer(t, inv_freq)] * 2, axis=-1).astype(np.float32)
    sin = np.sin(freqs).astype(np.float32)
    cos = np.cos(freqs).astype(np.float32)
    sin_signed = sin.copy()
    sin_signed[:, : HD // 2] *= -1.0
    cosT = np.ascontiguousarray(cos.T)
    sinT = np.ascontiguousarray(sin_signed.T)
    ropet = np.concatenate(
        [cosT * np.float32(1.0 / 128.0), sinT * np.float32(1.0 / 128.0),
         cosT, sinT], axis=1).astype(np.float32)

    identf = np.eye(128, dtype=np.float32)
    oneh = np.zeros((8, 8 * 128), np.float32)
    for ee in range(8):
        oneh[ee, ee * 128:(ee + 1) * 128] = 1.0
    gwt = np.ascontiguousarray(gate_w.T)

    w1t = np.concatenate([w1.transpose(0, 2, 1),
                          sw1.T[None]], axis=0)
    w3t = np.concatenate([w3.transpose(0, 2, 1),
                          sw3.T[None]], axis=0)
    w2t = np.concatenate([w2.transpose(0, 2, 1),
                          sw2.T[None]], axis=0)
    w1t = np.ascontiguousarray(w1t, np.float32)
    w3t = np.ascontiguousarray(w3t, np.float32)
    w2t = np.ascontiguousarray(w2t, np.float32)

    shared = {
        "wq": wq, "wk": wk, "wv": wv, "wo": wo,
        "bq2": bq2, "bk2": bk2, "bv2": bv2, "bo2": bo2,
        "ropet": ropet, "identf": identf, "oneh": oneh, "gwt": gwt,
        "w1t": w1t, "w3t": w3t, "w2t": w2t,
    }
    in_maps = []
    for b in range(8):
        m = dict(shared)
        m["xt"] = np.ascontiguousarray(x[b].T)
        in_maps.append(m)
    return in_maps


def kernel(**inputs):
    if "nc" not in _CACHE:
        _CACHE["nc"] = build_program()
    nc = _CACHE["nc"]
    in_maps = _host_prep(inputs)
    res = run_bass_kernel_spmd(nc, in_maps, core_ids=list(range(8)))
    out = np.stack([np.asarray(r["yt"], np.float32).T for r in res.results], axis=0)
    return out


# version 11
# speedup vs baseline: 1.1748x; 1.1748x over previous
import numpy as np
from contextlib import ExitStack

import concourse.bass as bass
import concourse.mybir as mybir
from concourse.bass_utils import run_bass_kernel_spmd

F32 = mybir.dt.float32
BF16 = mybir.dt.bfloat16
AF = mybir.ActivationFunctionType
ALU = mybir.AluOpType

D = 2048
T = 2048
H = 16
G = 2
HD = 128
HID = 1024
NE = 9
DC = D // 128
HC = HID // 128

_CACHE = {}


class Ctr:
    def __init__(self, nc, stack, name):
        self.sem = stack.enter_context(nc.semaphore(name))
        self.name = name
        self.n = 0

    def inc(self, instr, k=1):
        instr.then_inc(self.sem, k)
        self.n += k
        return self.n


class Eng:
    def __init__(self, raw, name):
        self.raw = raw
        self.name = name
        self.known = {}

    def wait(self, ctr, val):
        if val is None or val <= 0:
            return
        if self.known.get(ctr.name, -1) >= val:
            return
        self.raw.wait_ge(ctr.sem, val)
        self.known[ctr.name] = val


class OpWrap:

    COMPUTE = {
        "tensor_copy", "tensor_tensor", "tensor_add", "tensor_sub",
        "tensor_mul", "tensor_max", "tensor_scalar", "tensor_single_scalar",
        "tensor_scalar_mul", "tensor_scalar_add", "scalar_tensor_tensor",
        "tensor_reduce", "reduce_max", "reduce_sum", "reciprocal", "memset",
        "activation", "copy", "mul", "add", "square", "sqrt", "sign",
        "select", "stream_shuffle", "tensor_relu", "memzero",
    }

    def __init__(self, raw, weng, ctr):
        self._raw = raw
        self._weng = weng
        self._ctr = ctr

    def __getattr__(self, name):
        fn = getattr(self._raw, name)
        if name not in self.COMPUTE:
            return fn

        def wrapped(*a, **k):
            self._weng.wait(self._ctr, self._ctr.n)
            instr = fn(*a, **k)
            self._ctr.inc(instr)
            return instr

        return wrapped


def build_program():
    nc = bass.Bass("TRN2", target_bir_lowering=False)

    xt = nc.dram_tensor("xt", [D, T], F32, kind="ExternalInput").ap()
    wq = nc.dram_tensor("wq", [D, D], F32, kind="ExternalInput").ap()
    wk = nc.dram_tensor("wk", [D, G * HD], F32, kind="ExternalInput").ap()
    wv = nc.dram_tensor("wv", [D, G * HD], F32, kind="ExternalInput").ap()
    wo = nc.dram_tensor("wo", [D, D], F32, kind="ExternalInput").ap()
    bq2 = nc.dram_tensor("bq2", [128, H], F32, kind="ExternalInput").ap()
    bk2 = nc.dram_tensor("bk2", [128, G], F32, kind="ExternalInput").ap()
    bv2 = nc.dram_tensor("bv2", [128, G], F32, kind="ExternalInput").ap()
    bo2 = nc.dram_tensor("bo2", [128, DC], F32, kind="ExternalInput").ap()
    ropet = nc.dram_tensor("ropet", [128, 4 * T], F32, kind="ExternalInput").ap()
    identf = nc.dram_tensor("identf", [128, 128], F32, kind="ExternalInput").ap()
    oneh = nc.dram_tensor("oneh", [8, 8 * 128], F32, kind="ExternalInput").ap()
    gwt = nc.dram_tensor("gwt", [D, 8], F32, kind="ExternalInput").ap()
    w1t = nc.dram_tensor("w1t", [NE, D, HID], F32, kind="ExternalInput").ap()
    w3t = nc.dram_tensor("w3t", [NE, D, HID], F32, kind="ExternalInput").ap()
    w2t = nc.dram_tensor("w2t", [NE, HID, D], F32, kind="ExternalInput").ap()
    yt = nc.dram_tensor("yt", [D, T], F32, kind="ExternalOutput").ap()

    s_qt = nc.dram_tensor("s_qt", [H, 128, T], BF16, kind="Internal").ap()
    s_ao = nc.dram_tensor("s_ao", [D, T], BF16, kind="Internal").ap()
    s_xmid = nc.dram_tensor("s_xmid", [D, T], F32, kind="Internal").ap()
    s_xn2 = nc.dram_tensor("s_xn2", [D, T], BF16, kind="Internal").ap()
    s_w13 = nc.dram_tensor("s_w13", [NE, 2, D, HID], BF16, kind="Internal").ap()
    s_w2c = nc.dram_tensor("s_w2c", [NE, HID, D], BF16, kind="Internal").ap()
    s_cf = nc.dram_tensor("s_cf", [4, 32, 128], BF16, kind="Internal").ap()

    with ExitStack() as top:
        e = top.enter_context

        kv_scope = ExitStack()
        kt = kv_scope.enter_context(
            nc.sbuf_tensor("kt", [128, G * T], BF16, side="right"))
        vtm = kv_scope.enter_context(
            nc.sbuf_tensor("vtm", [128, G * 16 * 128], BF16, side="right"))
        bq2s = e(nc.sbuf_tensor("bq2s", [128, H], F32))
        bk2s = e(nc.sbuf_tensor("bk2s", [128, G], F32))
        bv2s = e(nc.sbuf_tensor("bv2s", [128, G], F32))
        bo2s = e(nc.sbuf_tensor("bo2s", [128, DC], F32))
        onesc = e(nc.sbuf_tensor("onesc", [128, 1], BF16))
        onesr = e(nc.sbuf_tensor("onesr", [1, 128], BF16))
        idf = e(nc.sbuf_tensor("idf", [128, 128], F32))
        gwbf = e(nc.sbuf_tensor("gwbf", [128, DC * 8], BF16))
        epsb = e(nc.sbuf_tensor("epsb", [1, 1], F32))
        rows = e(nc.sbuf_tensor("rows", [1, 2560], F32))
        rowsb = e(nc.sbuf_tensor("rowsb", [1, 1024], BF16))

        pb = [e(nc.psum_tensor(f"pb{i}", [128, 512], F32)) for i in range(8)]

        cP = Ctr(nc, top, "cP")
        cA = Ctr(nc, top, "cA")
        cV = Ctr(nc, top, "cV")
        d_cn = Ctr(nc, top, "d_cn")
        d_wa = Ctr(nc, top, "d_wa")
        d_kv = Ctr(nc, top, "d_kv")
        d_wq = [Ctr(nc, top, f"d_wq{i}") for i in range(4)]
        d_oh = Ctr(nc, top, "d_oh")
        d_wc = Ctr(nc, top, "d_wc")
        d_xf = [Ctr(nc, top, f"d_xf{i}") for i in range(2)]
        d_rp = [Ctr(nc, top, f"d_rp{i}") for i in range(2)]
        d_qs = [Ctr(nc, top, f"d_qs{i}") for i in range(2)]
        d_qh = [Ctr(nc, top, f"d_qh{i}") for i in range(2)]
        d_ao = [Ctr(nc, top, f"d_ao{i}") for i in range(2)]
        d_ac = [Ctr(nc, top, f"d_ac{i}") for i in range(2)]
        d_xc = [Ctr(nc, top, f"d_xc{i}") for i in range(2)]
        d_st = [Ctr(nc, top, f"d_st{i}") for i in range(2)]
        d_q3 = Ctr(nc, top, "d_q3")
        d_cf = Ctr(nc, top, "d_cf")
        d_qe = Ctr(nc, top, "d_qe")
        d_acc = Ctr(nc, top, "d_acc")
        d_yt = Ctr(nc, top, "d_yt")
        d_w1 = Ctr(nc, top, "d_w1")
        d_w3 = Ctr(nc, top, "d_w3")
        d_w2 = [Ctr(nc, top, f"d_w2{i}") for i in range(3)]

        TE = Eng(nc.tensor, "TE")
        SC = Eng(nc.scalar, "SC")
        VE = Eng(nc.vector, "VE")
        SY = Eng(nc.sync, "SY")
        GP = Eng(nc.gpsimd, "GP")
        NV = OpWrap(nc.vector, VE, cV)
        NS = OpWrap(nc.scalar, SC, cA)
        cQ = Ctr(nc, top, "cQ")
        NQ = OpWrap(nc.gpsimd, GP, cQ)

        d_cn.inc(nc.sync.dma_start(bq2s[:], bq2), 16)
        d_cn.inc(nc.sync.dma_start(bk2s[:], bk2), 16)
        d_cn.inc(nc.sync.dma_start(bv2s[:], bv2), 16)
        d_cn.inc(nc.sync.dma_start(bo2s[:], bo2), 16)
        d_cn.inc(nc.sync.dma_start(idf[:], identf), 16)
        d_wa.inc(nc.gpsimd.dma_start(
            gwbf[:].rearrange("p (c e) -> p c e", e=8),
            gwt.rearrange("(c p) e -> p c e", p=128)), 16)
        NV.memset(onesc[:], 1.0)
        NV.memset(onesr[:], 1.0)
        NV.memset(epsb[:], 1e-5)
        v_consts = cV.n

        def load_w_attn(dst, src, out_w):
            return nc.gpsimd.dma_start(
                dst[:].rearrange("p (c o) -> p c o", o=out_w),
                src.rearrange("(c p) o -> p c o", p=128))

        cache_jobs = []
        for ee in range(NE):
            cache_jobs.append((s_w13[ee, 0], w1t[ee]))
            cache_jobs.append((s_w13[ee, 1], w3t[ee]))
            cache_jobs.append((s_w2c[ee], w2t[ee]))
        cache_state = {"i": 0}

        def emit_cache(n):
            for _ in range(n):
                if cache_state["i"] < len(cache_jobs):
                    dst, src = cache_jobs[cache_state["i"]]
                    d_wc.inc(nc.gpsimd.dma_start(dst, src), 16)
                    cache_state["i"] += 1

        all_dma_ctrs = ([d_cn, d_wa, d_kv, d_oh, d_wc, d_qs[0], d_qs[1],
                         d_q3, d_cf, d_qe, d_acc, d_yt, d_w1, d_w3] + d_wq +
                        d_xf + d_rp + d_qh + d_ao + d_ac + d_xc + d_st + d_w2)

        def phase_barrier():
            for eng in (TE, SC, VE, SY, GP):
                eng.wait(cP, cP.n)
                eng.wait(cA, cA.n)
                eng.wait(cV, cV.n)
                eng.wait(cQ, cQ.n)
                for dctr in all_dma_ctrs:
                    eng.wait(dctr, dctr.n)

        ln_state = {"stat_free": [], "bc_free": []}

        def emit_layernorm(xsrc, dst_bf, ct, xbt, sqt, tmp1, ld_wait):
            p_dc = [0] * DC
            for dc in range(DC):
                bbb = dc % len(xbt)
                if ld_wait is not None:
                    VE.wait(ld_wait[0], ld_wait[1])
                if dc >= len(xbt):
                    VE.wait(cP, p_dc[dc - len(xbt)])
                NV.tensor_copy(xbt[bbb][:, 0:ct],
                                             xsrc[:, dc * ct:(dc + 1) * ct])
                NV.tensor_mul(sqt[bbb][:, 0:ct], xbt[bbb][:, 0:ct],
                                            xbt[bbb][:, 0:ct])
                v_sq = cV.n
                TE.wait(cV, v_sq)
                if dc == 0:
                    for wctr, wval in ln_state["stat_free"]:
                        TE.wait(wctr, wval)
                nc.tensor.matmul(pb[6][0:1, 0:ct], onesc[:], xbt[bbb][:, 0:ct],
                                 start=(dc == 0), stop=(dc == DC - 1))
                mm = nc.tensor.matmul(pb[7][0:1, 0:ct], onesc[:], sqt[bbb][:, 0:ct],
                                      start=(dc == 0), stop=(dc == DC - 1))
                cP.inc(mm)
                p_dc[dc] = cP.n
            p_stats = cP.n
            m_sl = rows[0:1, 0:ct]
            m2_sl = rows[0:1, ct:2 * ct]
            t_sl = rows[0:1, 2 * ct:3 * ct]
            v_sl = rows[0:1, 3 * ct:4 * ct]
            r_sl = rows[0:1, 4 * ct:5 * ct]
            VE.wait(cP, p_stats)
            NV.tensor_scalar_mul(m_sl, pb[6][0:1, 0:ct], 1.0 / D)
            NV.tensor_scalar_mul(m2_sl, pb[7][0:1, 0:ct], 1.0 / D)
            NV.tensor_mul(t_sl, m_sl, m_sl)
            NV.tensor_sub(v_sl, m2_sl, t_sl)
            ln_state["stat_free"] = [(cV, cV.n)]
            SC.wait(cV, cV.n)
            NS.activation(t_sl, v_sl, AF.Ln, bias=epsb[0:1, 0:1])
            NS.activation(r_sl, t_sl, AF.Exp, scale=-0.5)
            VE.wait(cA, cA.n)
            NV.tensor_copy(rowsb[0:1, 0:ct], m_sl)
            NV.tensor_copy(rowsb[0:1, ct:2 * ct], r_sl)
            v_mrb = cV.n
            TE.wait(cV, v_mrb)
            for wctr, wval in ln_state["bc_free"]:
                TE.wait(wctr, wval)
            nc.tensor.matmul(pb[4][:, 0:ct], onesr[:], rowsb[0:1, 0:ct],
                             start=True, stop=True)
            mm = nc.tensor.matmul(pb[5][:, 0:ct], onesr[:], rowsb[0:1, ct:2 * ct],
                                  start=True, stop=True)
            cP.inc(mm)
            VE.wait(cP, cP.n)
            for dc in range(DC):
                if tmp1 is None:
                    NV.tensor_sub(xsrc[:, dc * ct:(dc + 1) * ct],
                                  xsrc[:, dc * ct:(dc + 1) * ct], pb[4][:, 0:ct])
                    NV.tensor_mul(dst_bf[:, dc * ct:(dc + 1) * ct],
                                  xsrc[:, dc * ct:(dc + 1) * ct], pb[5][:, 0:ct])
                else:
                    bbb = dc % 2
                    NV.tensor_sub(tmp1[bbb][:, 0:ct],
                                  xsrc[:, dc * ct:(dc + 1) * ct], pb[4][:, 0:ct])
                    NV.tensor_mul(dst_bf[:, dc * ct:(dc + 1) * ct],
                                  tmp1[bbb][:, 0:ct], pb[5][:, 0:ct])
            ln_state["bc_free"] = [(cV, cV.n)]
            return cV.n

        with ExitStack() as ph1:
            e1 = ph1.enter_context
            wqbf = e1(nc.sbuf_tensor("wqbf", [128, DC * D], BF16))
            wkbf = e1(nc.sbuf_tensor("wkbf", [128, DC * G * HD], BF16))
            wvbf = e1(nc.sbuf_tensor("wvbf", [128, DC * G * HD], BF16))
            d_kv.inc(load_w_attn(wkbf, wk, G * HD), 16)
            d_kv.inc(load_w_attn(wvbf, wv, G * HD), 16)
            for qg in range(4):
                d_wq[qg].inc(nc.gpsimd.dma_start(
                    wqbf[:].rearrange("p (c o) -> p c o", o=D)
                    [:, :, qg * 512:(qg + 1) * 512],
                    wq.rearrange("(c p) o -> p c o", p=128)
                    [:, :, qg * 512:(qg + 1) * 512]), 16)
            CT = 512
            xf = e1(nc.sbuf_tensor("xf", [128, DC * CT], F32))
            rp = [e1(nc.sbuf_tensor(f"rp{i}", [128, 4 * CT], F32))
                  for i in range(2)]
            xbt = [e1(nc.sbuf_tensor(f"xbt{i}", [128, CT], BF16)) for i in range(3)]
            sqt = [e1(nc.sbuf_tensor(f"sqt{i}", [128, CT], BF16)) for i in range(3)]
            xn = [e1(nc.sbuf_tensor(f"xn{i}", [128, DC * CT], BF16)) for i in range(2)]
            bcm = e1(nc.sbuf_tensor("bcm", [128, CT], BF16))
            bcr = e1(nc.sbuf_tensor("bcr", [128, CT], BF16))
            qst = [e1(nc.sbuf_tensor(f"qst{i}", [128, CT], BF16)) for i in range(2)]
            qf = [e1(nc.sbuf_tensor(f"qf{i}", [128, CT], F32)) for i in range(2)]
            rot = [e1(nc.sbuf_tensor(f"rot{i}", [128, CT], F32)) for i in range(2)]

            st1 = {
                "rp_free": [0, 0],
                "xf_free_v": 0,
                "xf_half_q": 0,
                "xf_half_v": 0,
                "stat_done": 0,
                "bank_free": [(None, 0)] * 4,
                "evac_free": [(None, 0), (None, 0)],
                "xn_free": [0, 0],
                "qst_free": [0, 0],
                "bc_sb_free": [0, 0],
                "lds": {},
            }
            VE.wait(cV, v_consts)
            SC.wait(d_cn, d_cn.n)

            def p1_loads(tc):
                xpar = tc % 2
                tok0 = tc * CT
                SY.wait(cV, st1["xf_free_v"])
                SY.wait(cQ, st1["xf_half_q"])
                ld0a = d_xf[0].inc(nc.sync.dma_start(
                    xf[:, 0:8 * CT].rearrange("p (c t) -> p c t", t=CT),
                    xt[0:1024, tok0:tok0 + CT]
                    .rearrange("(c p) t -> p c t", p=128)), 16)
                SY.wait(cQ, st1.get("xf_full_q", 0))
                ld0b = d_xf[1].inc(nc.sync.dma_start(
                    xf[:, 8 * CT:16 * CT].rearrange("p (c t) -> p c t", t=CT),
                    xt[1024:2048, tok0:tok0 + CT]
                    .rearrange("(c p) t -> p c t", p=128)), 16)
                SY.wait(cV, st1["rp_free"][xpar])
                ld1 = d_rp[xpar].inc(nc.sync.dma_start(
                    rp[xpar][:].rearrange("p (f t) -> p f t", t=CT),
                    ropet.rearrange("p (f t) -> p f t", t=T)
                    [:, :, tok0:tok0 + CT]), 16)
                st1["lds"][tc] = (ld0a, ld0b, ld1)

            def emit_ln(tc):
                xpar = tc % 2
                ld0a, ld0b, ld1 = st1["lds"][tc]
                p_sq = [0] * DC
                for dc in range(DC):
                    VE.wait(d_xf[0] if dc < 8 else d_xf[1],
                            ld0a if dc < 8 else ld0b)
                    spp = dc % 3
                    if dc >= 3:
                        VE.wait(cP, p_sq[dc - 3])
                    NV.tensor_copy(xbt[spp][:], xf[:, dc * CT:(dc + 1) * CT])
                    NV.tensor_mul(sqt[spp][:], xbt[spp][:], xbt[spp][:])
                    v_sq = cV.n
                    TE.wait(cV, v_sq)
                    if dc == 0:
                        for wctr, wval in ln_state["stat_free"]:
                            TE.wait(wctr, wval)
                    nc.tensor.matmul(pb[6][0:1, 0:CT], onesc[:], xbt[spp][:],
                                     start=(dc == 0), stop=(dc == DC - 1))
                    mm = nc.tensor.matmul(pb[7][0:1, 0:CT], onesc[:], sqt[spp][:],
                                          start=(dc == 0), stop=(dc == DC - 1))
                    cP.inc(mm)
                    p_sq[dc] = cP.n
                st1["stat_done"] = cP.n
                st1["xf_free_v"] = cV.n
                ct = CT
                m_sl = rows[0:1, 0:ct]
                m2_sl = rows[0:1, ct:2 * ct]
                t_sl = rows[0:1, 2 * ct:3 * ct]
                v_sl = rows[0:1, 3 * ct:4 * ct]
                r_sl = rows[0:1, 4 * ct:5 * ct]
                VE.wait(cP, st1["stat_done"])
                NV.tensor_scalar_mul(m_sl, pb[6][0:1, 0:ct], 1.0 / D)
                NV.tensor_scalar_mul(m2_sl, pb[7][0:1, 0:ct], 1.0 / D)
                NV.tensor_mul(t_sl, m_sl, m_sl)
                NV.tensor_sub(v_sl, m2_sl, t_sl)
                ln_state["stat_free"] = [(cV, cV.n)]
                SC.wait(cV, cV.n)
                NS.activation(t_sl, v_sl, AF.Ln, bias=epsb[0:1, 0:1])
                NS.activation(r_sl, t_sl, AF.Exp, scale=-0.5)
                VE.wait(cA, cA.n)
                NV.tensor_copy(rowsb[0:1, 0:ct], m_sl)
                NV.tensor_copy(rowsb[0:1, ct:2 * ct], r_sl)
                v_mrb = cV.n
                TE.wait(cV, v_mrb)
                for wctr, wval in ln_state["bc_free"]:
                    TE.wait(wctr, wval)
                nc.tensor.matmul(pb[4][:, 0:ct], onesr[:], rowsb[0:1, 0:ct],
                                 start=True, stop=True)
                mm = nc.tensor.matmul(pb[5][:, 0:ct], onesr[:],
                                      rowsb[0:1, ct:2 * ct],
                                      start=True, stop=True)
                cP.inc(mm)
                p_bc = cP.n
                SC.wait(cP, p_bc)
                SC.wait(cQ, st1["bc_sb_free"][0])
                NS.copy(bcm[:], pb[4][:, 0:ct])
                NS.copy(bcr[:], pb[5][:, 0:ct])
                a_bc = cA.n
                GP.wait(cA, a_bc)
                GP.wait(cP, st1["xn_free"][xpar])
                q_half = 0
                for dc in range(DC):
                    GP.wait(d_xf[0] if dc < 8 else d_xf[1],
                            ld0a if dc < 8 else ld0b)
                    NQ.tensor_sub(xf[:, dc * CT:(dc + 1) * CT],
                                  xf[:, dc * CT:(dc + 1) * CT], bcm[:])
                    NQ.tensor_mul(xn[xpar][:, dc * CT:(dc + 1) * CT],
                                  xf[:, dc * CT:(dc + 1) * CT], bcr[:])
                    if dc == 7:
                        q_half = cQ.n
                st1["xf_half_v"] = 0
                st1["xf_half_q"] = q_half
                st1["xf_full_q"] = cQ.n
                st1["bc_sb_free"] = (cQ.n, 0)
                ln_state["bc_free"] = [(cQ, cQ.n)]
                st1["q_xn", tc] = (0, cQ.n)

            def emit_qkv_chain(tc, oi, kind, idx):
                xpar = tc % 2
                tok0 = tc * CT
                bank = oi % 3
                vq, qq_ = st1["q_xn", tc]
                TE.wait(cV, vq)
                TE.wait(cQ, qq_)
                if kind == "q":
                    TE.wait(d_wq[idx // 4], 16)
                else:
                    TE.wait(d_kv, d_kv.n)
                wctr, wval = st1["bank_free"][bank]
                if wctr is not None:
                    TE.wait(wctr, wval)
                for dc in range(DC):
                    if kind == "q":
                        lhs = wqbf[:, dc * D + idx * 128: dc * D + (idx + 1) * 128]
                    elif kind == "k":
                        lhs = wkbf[:, dc * G * HD + idx * 128:
                                   dc * G * HD + (idx + 1) * 128]
                    else:
                        lhs = wvbf[:, dc * G * HD + idx * 128:
                                   dc * G * HD + (idx + 1) * 128]
                    mm = nc.tensor.matmul(pb[bank][:, 0:CT], lhs,
                                          xn[xpar][:, dc * CT:(dc + 1) * CT],
                                          start=(dc == 0), stop=(dc == DC - 1))
                cP.inc(mm)
                p_oc = cP.n
                if oi == 19:
                    st1["xn_free"][xpar] = p_oc
                r = oi % 2
                SC.wait(cP, p_oc)
                pctr, pval = st1["evac_free"][r]
                if pctr is not None:
                    SC.wait(pctr, pval)
                if kind == "q":
                    bias = bq2s[:, idx:idx + 1]
                elif kind == "k":
                    bias = bk2s[:, idx:idx + 1]
                else:
                    bias = bv2s[:, idx:idx + 1]
                NS.activation(qf[r][:], pb[bank][:, 0:CT], AF.Identity, bias=bias)
                st1["bank_free"][bank] = (cA, cA.n)
                a_evac = cA.n
                if kind in ("q", "k"):
                    NS.copy(rot[r][0:64, :], qf[r][64:128, :])
                    NS.copy(rot[r][64:128, :], qf[r][0:64, :])
                    a_rot = cA.n
                    co = 0 if kind == "q" else 2 * CT
                    si = CT if kind == "q" else 3 * CT
                    xpar2 = tc % 2
                    VE.wait(cA, a_rot)
                    VE.wait(d_rp[xpar2], st1["lds"][tc][2])
                    NV.tensor_mul(qf[r][:], qf[r][:], rp[xpar2][:, co:co + CT])
                    NV.tensor_mul(rot[r][:], rot[r][:], rp[xpar2][:, si:si + CT])
                    if kind == "q":
                        qr2 = r
                        VE.wait(d_qs[qr2], st1["qst_free"][qr2])
                        NV.tensor_add(qst[qr2][:], qf[r][:], rot[r][:])
                        v_q = cV.n
                        SY.wait(cV, v_q)
                        st1["qst_free"][qr2] = d_qs[qr2].inc(nc.sync.dma_start(
                            s_qt[idx, :, tok0:tok0 + CT], qst[qr2][:]), 16)
                        st1["evac_free"][r] = (cV, cV.n)
                    else:
                        dst = kt[:, idx * T + tok0: idx * T + tok0 + CT]
                        NV.tensor_add(dst, qf[r][:], rot[r][:])
                        st1["evac_free"][r] = (cV, cV.n)
                else:
                    for half in range(4):
                        kc = tc * 4 + half
                        TE.wait(cA, a_evac)
                        TE.wait(d_cn, d_cn.n)
                        wctr2, wval2 = st1["bank_free"][3]
                        if wctr2 is not None:
                            TE.wait(wctr2, wval2)
                        cP.inc(nc.tensor.transpose(
                            pb[3][0:128, 0:128],
                            qf[r][:, half * 128:(half + 1) * 128], idf[:]))
                        p_tr = cP.n
                        VE.wait(cP, p_tr)
                        col = (idx * 16 + kc) * 128
                        NV.tensor_copy(vtm[:, col:col + 128],
                                       pb[3][0:128, 0:128])
                        st1["bank_free"][3] = (cV, cV.n)
                    st1["evac_free"][r] = (cV, cV.n)
                if oi == 19:
                    st1["rp_free"][tc % 2] = cV.n

            oc_list = [("k", 0), ("k", 1), ("v", 0), ("v", 1)] + \
                      [("q", hh) for hh in range(H)]
            p1_loads(0)
            emit_ln(0)
            for tc in range(4):
                if tc + 1 < 4:
                    p1_loads(tc + 1)
                for oi in range(8):
                    emit_qkv_chain(tc, oi, *oc_list[oi])
                if tc + 1 < 4:
                    emit_ln(tc + 1)
                for oi in range(8, 20):
                    emit_qkv_chain(tc, oi, *oc_list[oi])
                emit_cache(2)

        p1_vend = cV.n
        p1_pend = cP.n
        phase_barrier()
        wo_scope = ExitStack()
        wobf = wo_scope.enter_context(nc.sbuf_tensor("wobf", [128, DC * D], BF16))
        with ExitStack() as ph2a:
            e2 = ph2a.enter_context
            qh = [e2(nc.sbuf_tensor(f"qh{i}", [128, T], BF16)) for i in range(2)]
            et = e2(nc.sbuf_tensor("et", [128, 16 * 512], BF16))
            ao = [e2(nc.sbuf_tensor(f"ao{i}", [128, 512], BF16)) for i in range(2)]
            rsb = e2(nc.sbuf_tensor("rsb", [128, 512], F32))

            GP.wait(cP, p1_pend)
            d_wo = d_wa.inc(load_w_attn(wobf, wo, D), 16)

            qh_free = [0, 0]
            qh_ld = [0, 0]
            sc_bank_free = [0, 0, 0, 0]
            od_free = [[0, 0], [0, 0]]
            et_free = [0] * 16
            ao_free = [0, 0]

            TE.wait(cV, p1_vend)
            SY.wait(d_qs[0], d_qs[0].n)
            SY.wait(d_qs[1], d_qs[1].n)
            qh_ld[0] = d_qh[0].inc(nc.sync.dma_start(qh[0][:], s_qt[0]), 16)

            fin_pend = None

            rec_free = [0]

            def p2a_finish(fin):
                (h, qc, q0, par2, ob, db, p_avd, b) = fin
                SC.wait(cP, p_avd)
                SC.wait(cV, rec_free[0])
                NS.activation(rows[0:1, 0:512], pb[db][0:1, 0:512], AF.Ln)
                NS.activation(rows[0:1, 512:1024], rows[0:1, 0:512],
                              AF.Exp, scale=-1.0)
                a_rec = cA.n
                VE.wait(cA, a_rec)
                NV.tensor_copy(rowsb[0:1, 0:512], rows[0:1, 512:1024])
                v_rec = cV.n
                rec_free[0] = v_rec
                TE.wait(cV, v_rec)
                TE.wait(cA, a_rec)
                cP.inc(nc.tensor.matmul(pb[db][:, 0:512], onesr[:],
                                        rowsb[0:1, 0:512], start=True, stop=True))
                p_r = cP.n
                VE.wait(cP, p_r)
                NV.tensor_copy(rsb[:], pb[db][:, 0:512])
                od_free[par2][1] = cV.n
                b2 = (h * 4 + qc) % 2
                VE.wait(d_ao[b2], ao_free[b2])
                NV.tensor_mul(ao[b2][:], pb[ob][:, 0:512], rsb[:])
                od_free[par2][0] = cV.n
                v_ao = cV.n
                SY.wait(cV, v_ao)
                ao_free[b2] = d_ao[b2].inc(nc.sync.dma_start(
                    s_ao[h * 128:(h + 1) * 128, q0:q0 + 512], ao[b2][:]), 16)

            for h in range(H):
                b = h % 2
                g = h // (H // G)
                if h + 1 < H:
                    SY.wait(cP, qh_free[(h + 1) % 2])
                    qh_ld[(h + 1) % 2] = d_qh[(h + 1) % 2].inc(
                        nc.sync.dma_start(qh[(h + 1) % 2][:], s_qt[h + 1]), 16)
                for qc in range(4):
                    emit_cache(1)
                    q0 = qc * 512
                    qidx = h * 4 + qc
                    par2 = qidx % 2
                    ob, db = 4 + 2 * par2, 5 + 2 * par2
                    p_s = [0] * 16
                    a_e = [0] * 16
                    TE.wait(d_qh[b], qh_ld[b])

                    def emit_score(kc):
                        bank = kc % 4
                        TE.wait(cA, sc_bank_free[bank])
                        cP.inc(nc.tensor.matmul(
                            pb[bank][:, 0:512],
                            kt[:, g * T + kc * 128: g * T + (kc + 1) * 128],
                            qh[b][:, q0:q0 + 512], start=True, stop=True))
                        p_s[kc] = cP.n

                    for kc in range(4):
                        emit_score(kc)
                    for kc in range(16):
                        bank = kc % 4
                        SC.wait(cP, p_s[kc])
                        SC.wait(cP, et_free[kc])
                        NS.activation(
                            et[:, kc * 512:(kc + 1) * 512], pb[bank][:, 0:512],
                            AF.Exp)
                        a_e[kc] = cA.n
                        sc_bank_free[bank] = a_e[kc]
                        if kc + 4 < 16:
                            emit_score(kc + 4)
                        TE.wait(cA, a_e[kc])
                        if kc == 0:
                            TE.wait(cV, od_free[par2][0])
                            TE.wait(cV, od_free[par2][1])
                        col = (g * 16 + kc) * 128
                        nc.tensor.matmul(pb[ob][:, 0:512], vtm[:, col:col + 128],
                                         et[:, kc * 512:(kc + 1) * 512],
                                         start=(kc == 0), stop=(kc == 15))
                        mm = nc.tensor.matmul(pb[db][0:1, 0:512], onesc[:],
                                              et[:, kc * 512:(kc + 1) * 512],
                                              start=(kc == 0), stop=(kc == 15))
                    cP.inc(mm)
                    p_avd = cP.n
                    for kc in range(16):
                        et_free[kc] = p_avd
                    if qc == 3:
                        qh_free[b] = p_avd
                    if fin_pend is not None:
                        p2a_finish(fin_pend)
                    fin_pend = (h, qc, q0, par2, ob, db, p_avd, b)
            p2a_finish(fin_pend)
            fin_pend = None
            emit_cache(99)
            ao_store_all = (d_ao[0].n, d_ao[1].n)
        kv_scope.close()

        phase_barrier()
        with ExitStack() as ph2b:
            e3 = ph2b.enter_context
            CT = 512
            aoc = [e3(nc.sbuf_tensor("aoc0", [128, DC * CT], BF16))]
            xfc = [e3(nc.sbuf_tensor(f"xfc{i}", [128, DC * CT], F32)) for i in range(2)]
            xbt = [e3(nc.sbuf_tensor("xbt2_0", [128, CT], BF16))]
            sqt = [e3(nc.sbuf_tensor("sqt2_0", [128, CT], BF16))]
            xn2b = [e3(nc.sbuf_tensor(f"xn2b{i}", [128, DC * CT], BF16))
                    for i in range(2)]

            sctm = e3(nc.sbuf_tensor("sctm", [128, 32], F32))
            sgt = sctm
            rank = e3(nc.sbuf_tensor("rank", [128, 32], F32))
            maskt = e3(nc.sbuf_tensor("maskt", [128, 32], F32))
            cbf32 = [e3(nc.sbuf_tensor(f"cbf32_{i}", [128, 32], F32,
                                        side="right"))
                     for i in range(2)]
            cbsb = e3(nc.sbuf_tensor("cbsb", [32, 128], BF16, side="right"))
            gate_state = {"cb_free": 0, "sctm_free": 0, "g_bank_a": 0,
                          "v_c": [0, 0], "p_g_last": 0}

            aoc_free = [0]
            xfc_free = [0, 0]
            wo_bank_free = [0, 0, 0]
            xn2_free = 0
            v_xmid = [0] * 4
            ln_state["stat_free"] = [(cV, cV.n)]
            ln_state["bc_free"] = [(cV, cV.n)]
            SY.wait(d_ao[0], ao_store_all[0])
            SY.wait(d_ao[1], ao_store_all[1])

            def gate_logits(qq):
                gpar = qq % 2
                gq = qq % 2
                for tt in range(4):
                    TE.wait(cV, v_xn2_of[qq])
                    TE.wait(d_wa, d_wa.n)
                    for wctr, wval in ln_state["stat_free"]:
                        TE.wait(wctr, wval)
                    TE.wait(cA, gate_state["g_bank_a"])
                    for dc in range(DC):
                        mm = nc.tensor.matmul(
                            pb[6][:, 0:8],
                            xn2b[gq][:, dc * CT + tt * 128:
                                     dc * CT + (tt + 1) * 128],
                            gwbf[:, dc * 8:(dc + 1) * 8],
                            start=(dc == 0), stop=(dc == DC - 1))
                    cP.inc(mm)
                    p_g = cP.n
                    gate_state["p_g_last"] = p_g
                    SC.wait(cP, p_g)
                    SC.wait(cV, gate_state["sctm_free"])
                    NS.activation(sgt[:, tt * 8:(tt + 1) * 8],
                                  pb[6][:, 0:8], AF.Exp, scale=-1.0)
                    gate_state["g_bank_a"] = cA.n
                a_sg = cA.n
                ln_state["stat_free"] = (ln_state["stat_free"] +
                                         [(cA, gate_state["g_bank_a"])])
                VE.wait(cA, a_sg)
                NV.tensor_scalar_add(maskt[:], sgt[:], 1.0)
                NV.reciprocal(sctm[:], maskt[:])
                NV.memset(rank[:], 0.0)
                for j in range(8):
                    for tt in range(4):
                        if j < 7:
                            NV.scalar_tensor_tensor(
                                rank[:, tt * 8 + j + 1: tt * 8 + 8],
                                sctm[:, tt * 8 + j + 1: tt * 8 + 8],
                                sctm[:, tt * 8 + j: tt * 8 + j + 1],
                                rank[:, tt * 8 + j + 1: tt * 8 + 8],
                                ALU.is_le, ALU.add)
                        if j > 0:
                            NV.scalar_tensor_tensor(
                                rank[:, tt * 8: tt * 8 + j],
                                sctm[:, tt * 8: tt * 8 + j],
                                sctm[:, tt * 8 + j: tt * 8 + j + 1],
                                rank[:, tt * 8: tt * 8 + j],
                                ALU.is_lt, ALU.add)
                NV.tensor_single_scalar(maskt[:], rank[:], 2.0, ALU.is_lt)
                NV.tensor_mul(cbf32[gpar][:], maskt[:], sctm[:])
                gate_state["sctm_free"] = cV.n
                gate_state["v_c"][gpar] = cV.n

            def gate_finish(qq):
                gpar = qq % 2
                TE.wait(cV, gate_state["v_c"][gpar])
                for wctr, wval in ln_state["stat_free"]:
                    TE.wait(wctr, wval)
                cP.inc(nc.tensor.transpose(pb[7][0:32, 0:128],
                                           cbf32[gpar][:], idf[:]))
                p_t = cP.n
                VE.wait(cP, p_t)
                VE.wait(d_cf, gate_state["cb_free"])
                NV.tensor_copy(cbsb[:], pb[7][0:32, 0:128])
                v_cb = cV.n
                ln_state["stat_free"] = ln_state["stat_free"] + [(cV, v_cb)]
                SY.wait(cV, v_cb)
                cb_store = d_cf.inc(nc.sync.dma_start(s_cf[qq], cbsb[:]), 16)
                gate_state["cb_free"] = cb_store

            v_xn2_of = [0] * 4

            def p2b_load_ao(tc):
                tok0 = tc * CT
                SC.wait(cP, aoc_free[0])
                return d_ac[0].inc(nc.scalar.dma_start(
                    aoc[0][:].rearrange("p (c t) -> p c t", t=CT),
                    s_ao[:, tok0:tok0 + CT]
                    .rearrange("(c p) t -> p c t", p=128)), 16)

            def p2b_load_x(tc):
                b = tc % 2
                tok0 = tc * CT
                SC.wait(d_st[b], xfc_free[b])
                return d_xc[b].inc(nc.scalar.dma_start(
                    xfc[b][:].rearrange("p (c t) -> p c t", t=CT),
                    xt[:, tok0:tok0 + CT]
                    .rearrange("(c p) t -> p c t", p=128)), 16)

            lda_next = p2b_load_ao(0)
            ldx_next = p2b_load_x(0)
            for tc in range(7):
                pass_dummy = 0
                b = tc % 2
                if tc < 4:
                    tok0 = tc * CT
                    lda, ldx = lda_next, ldx_next
                    for oc in range(DC):
                        bank = oc % 3
                        TE.wait(d_ac[0], lda)
                        TE.wait(d_wa, d_wo)
                        TE.wait(cV, wo_bank_free[bank])
                        for dc in range(DC):
                            mm = nc.tensor.matmul(
                                pb[bank][:, 0:CT],
                                wobf[:, dc * D + oc * 128: dc * D + (oc + 1) * 128],
                                aoc[0][:, dc * CT:(dc + 1) * CT],
                                start=(dc == 0), stop=(dc == DC - 1))
                        cP.inc(mm)
                        p_oc = cP.n
                        if oc == DC - 1:
                            aoc_free[0] = p_oc
                        VE.wait(cP, p_oc)
                        VE.wait(d_xc[b], ldx)
                        NV.scalar_tensor_tensor(
                            xfc[b][:, oc * CT:(oc + 1) * CT],
                            pb[bank][:, 0:CT], bo2s[:, oc:oc + 1],
                            xfc[b][:, oc * CT:(oc + 1) * CT],
                            ALU.add, ALU.add)
                        wo_bank_free[bank] = cV.n
                    v_xmid[tc] = cV.n
                    if tc + 1 < 4:
                        lda_next = p2b_load_ao(tc + 1)
                    SY.wait(cV, v_xmid[tc])
                    d_st[b].inc(nc.sync.dma_start(
                        s_xmid[:, tok0:tok0 + CT]
                        .rearrange("(c p) t -> p c t", p=128),
                        xfc[b][:].rearrange("p (c t) -> p c t", t=CT)), 16)
                if 1 <= tc <= 4:
                    pb_ = (tc - 1) % 2
                    ptok = (tc - 1) * CT
                    if tc > 1:
                        VE.wait(d_st[tc % 2], xn2_free)
                    VE.wait(cV, v_xmid[tc - 1])
                    VE.wait(d_st[pb_], d_st[pb_].n)
                    VE.wait(cP, gate_state["p_g_last"])
                    v_xn2 = emit_layernorm(xfc[pb_], xn2b[pb_], CT,
                                           xbt, sqt, None, None)
                    v_xn2_of[tc - 1] = v_xn2
                    SY.wait(cV, v_xn2)
                    st2 = d_st[pb_].inc(nc.sync.dma_start(
                        s_xn2[:, ptok:ptok + CT]
                        .rearrange("(c p) t -> p c t", p=128),
                        xn2b[pb_][:].rearrange("p (c t) -> p c t", t=CT)), 16)
                    xn2_free = st2
                    xfc_free[pb_] = st2
                if 2 <= tc <= 5:
                    gate_logits(tc - 2)
                if tc + 1 < 4:
                    ldx_next = p2b_load_x(tc + 1)
                if 3 <= tc <= 6:
                    gate_finish(tc - 3)
        cft_ld = d_cf.n
        wo_scope.close()

        phase_barrier()
        with ExitStack() as ph3e:
            e5 = ph3e.enter_context
            ohbf = e5(nc.sbuf_tensor("ohbf", [8, 8 * 128], BF16))
            cft = e5(nc.sbuf_tensor("cft", [8, T], BF16))
            SY.wait(d_cf, d_cf.n)
            for qq4 in range(4):
                d_cf.inc(nc.sync.dma_start(
                    cft[:, qq4 * 512:(qq4 + 1) * 512]
                    .rearrange("e (tt t) -> e tt t", t=128),
                    s_cf[qq4].rearrange("(tt e) t -> e tt t", e=8)), 16)
            cft_ld = d_cf.n
            d_oh.inc(nc.gpsimd.dma_start(ohbf[:], oneh), 16)
            xn2q = e5(nc.sbuf_tensor("xn2q", [128, DC * 512], BF16))
            acc = e5(nc.sbuf_tensor("acc", [128, DC * 512], F32))
            xe = [e5(nc.sbuf_tensor(f"xe{i}", [128, DC * 512], BF16)) for i in range(2)]
            ball = e5(nc.sbuf_tensor("ball", [128, 8 * 512], BF16))
            ubf = e5(nc.sbuf_tensor("ubf", [128, HC * 512], BF16))
            hbf = ubf
            w1b = e5(nc.sbuf_tensor("w1b", [128, DC * HID], BF16))
            w3b = e5(nc.sbuf_tensor("w3b", [128, DC * HID], BF16))
            w2h = [e5(nc.sbuf_tensor(f"w2h{i}", [128, 4 * 1024], BF16))
                   for i in range(2)]

            bank_free = [("V", 0)] * 8
            xn2q_free = 0
            acc_free = 0
            xe_free = [0, 0]
            ball_free = 0
            ubf_free = 0
            hbf_free = 0
            w1_free = 0
            w3_free = 0
            w2h_free = [0, 0]
            w2h_ld = [0, 0]
            w1_ld = 0
            w3_ld = 0
            xe_pre = {}

            def wait_bank(engine, k):
                who, val = bank_free[k]
                engine.wait(cV if who == "V" else cA, val)

            eorder = [8, 0, 1, 2, 3, 4, 5, 6, 7]
            pairs = [(qq, ee) for qq in range(4) for ee in eorder]

            GP.wait(d_wc, d_wc.n)
            w1_ld = d_w1.inc(nc.gpsimd.dma_start(
                w1b[:].rearrange("p (c o) -> p c o", o=HID),
                s_w13[8, 0].rearrange("(c p) o -> p c o", p=128)), 16)
            w3_ld = d_w3.inc(nc.gpsimd.dma_start(
                w3b[:].rearrange("p (c o) -> p c o", o=HID),
                s_w13[8, 1].rearrange("(c p) o -> p c o", p=128)), 16)

            for pi, (qq, ee) in enumerate(pairs):
                t0 = qq * 512
                first_of_quarter = (ee == eorder[0])
                if first_of_quarter:
                    SY.wait(cP, xn2q_free)
                    SY.wait(d_cf, cft_ld)
                    ldq = d_qe.inc(nc.sync.dma_start(
                        xn2q[:].rearrange("p (c t) -> p c t", t=512),
                        s_xn2[:, t0:t0 + 512].rearrange("(c p) t -> p c t", p=128)), 16)
                    SY.wait(d_yt, acc_free)
                    lda = d_acc.inc(nc.sync.dma_start(
                        acc[:].rearrange("p (c t) -> p c t", t=512),
                        s_xmid[:, t0:t0 + 512].rearrange("(c p) t -> p c t", p=128)), 16)
                    SC.wait(cV, ball_free)
                    for ex in range(8):
                        TE.wait(d_cf, cft_ld)
                        TE.wait(d_oh, d_oh.n)
                        wait_bank(TE, 6)
                        cP.inc(nc.tensor.matmul(
                            pb[6][:, 0:512], ohbf[0:8, ex * 128:(ex + 1) * 128],
                            cft[0:8, t0:t0 + 512], start=True, stop=True))
                        p_b = cP.n
                        SC.wait(cP, p_b)
                        NS.copy(ball[:, ex * 512:(ex + 1) * 512],
                                              pb[6][:, 0:512])
                        bank_free[6] = ("A", cA.n)
                    a_ball = cA.n
                    ld_quarter = (ldq, lda)

                ldq, lda = ld_quarter
                par = pi % 2
                routed = ee != 8
                if routed:
                    v_xe = xe_pre.pop(pi)
                    rhs_buf = xe[par]
                else:
                    v_xe = 0
                    rhs_buf = xn2q

                def w2_load(ci):
                    p2c, hhc = divmod(ci, 2)
                    buf = ci % 2
                    GP.wait(cP, w2h_free[buf])
                    w2h_ld[buf] = d_w2[buf].inc(nc.gpsimd.dma_start(
                        w2h[buf][:].rearrange("p (c o) -> p c o", o=1024),
                        s_w2c[ee][hhc * 512:(hhc + 1) * 512,
                                  p2c * 1024:(p2c + 1) * 1024]
                        .rearrange("(c p) o -> p c o", p=128)), 16)
                w2_load(0)
                w2_load(1)

                TE.wait(d_w1, w1_ld)
                TE.wait(cV, v_xe)
                if not routed:
                    TE.wait(d_qe, ldq)
                for dc in range(DC):
                    for oc in range(HC):
                        if dc == 0:
                            wait_bank(TE, oc)
                        mm = nc.tensor.matmul(
                            pb[oc][:, 0:512],
                            w1b[:, dc * HID + oc * 128: dc * HID + (oc + 1) * 128],
                            rhs_buf[:, dc * 512:(dc + 1) * 512],
                            start=(dc == 0), stop=(dc == DC - 1))
                        if dc == DC - 1:
                            cP.inc(mm)
                p_a1 = cP.n
                w1_free = p_a1
                if pi + 1 < len(pairs):
                    nqq, nee = pairs[pi + 1]
                    GP.wait(cP, w1_free)
                    w1_ld_next = d_w1.inc(nc.gpsimd.dma_start(
                        w1b[:].rearrange("p (c o) -> p c o", o=HID),
                        s_w13[nee, 0].rearrange("(c p) o -> p c o", p=128)), 16)
                a_u = [0] * HC
                for oc in range(HC):
                    SC.wait(cP, p_a1 - 7 + oc)
                    SC.wait(cP, hbf_free)
                    NS.activation(ubf[:, oc * 512:(oc + 1) * 512],
                                                pb[oc][:, 0:512], AF.Silu)
                    a_u[oc] = cA.n
                    bank_free[oc] = ("A", cA.n)
                TE.wait(d_w3, w3_ld)
                for oc in range(HC):
                    TE.wait(cA, a_u[oc])
                    for dc in range(DC):
                        mm = nc.tensor.matmul(
                            pb[oc][:, 0:512],
                            w3b[:, dc * HID + oc * 128: dc * HID + (oc + 1) * 128],
                            rhs_buf[:, dc * 512:(dc + 1) * 512],
                            start=(dc == 0), stop=(dc == DC - 1))
                    cP.inc(mm)
                p_a3 = cP.n
                w3_free = p_a3
                xe_free[par] = p_a3
                if not routed:
                    pass
                if pi + 1 < len(pairs):
                    GP.wait(cP, w3_free)
                    w3_ld_next = d_w3.inc(nc.gpsimd.dma_start(
                        w3b[:].rearrange("p (c o) -> p c o", o=HID),
                        s_w13[nee, 1].rearrange("(c p) o -> p c o", p=128)), 16)
                v_h = [0] * HC
                for oc in range(HC):
                    VE.wait(cP, p_a3 - 7 + oc)
                    VE.wait(cP, hbf_free)
                    NV.tensor_mul(hbf[:, oc * 512:(oc + 1) * 512],
                                              ubf[:, oc * 512:(oc + 1) * 512],
                                              pb[oc][:, 0:512])
                    v_h[oc] = cV.n
                    bank_free[oc] = ("V", cV.n)
                ubf_free = cV.n
                if pi + 1 < len(pairs):
                    nqq2, nee2 = pairs[pi + 1]
                    if nee2 != 8 and nqq2 == qq:
                        npar = (pi + 1) % 2
                        VE.wait(d_qe, ldq)
                        VE.wait(cA, a_ball)
                        VE.wait(cP, xe_free[npar])
                        for dc in range(DC):
                            NV.tensor_mul(
                                xe[npar][:, dc * 512:(dc + 1) * 512],
                                xn2q[:, dc * 512:(dc + 1) * 512],
                                ball[:, nee2 * 512:(nee2 + 1) * 512])
                        xe_pre[pi + 1] = cV.n
                p_w2 = {}
                for p2 in range(2):
                    for hh in range(2):
                        ci = p2 * 2 + hh
                        buf = ci % 2
                        if ci >= 2:
                            w2_load(ci)
                        TE.wait(d_w2[buf], w2h_ld[buf])
                        for hc4 in range(4):
                            hc = hh * 4 + hc4
                            TE.wait(cV, v_h[hc])
                            for j in range(8):
                                if hc == 0:
                                    wait_bank(TE, j)
                                mm = nc.tensor.matmul(
                                    pb[j][:, 0:512],
                                    w2h[buf][:, hc4 * 1024 + j * 128:
                                             hc4 * 1024 + (j + 1) * 128],
                                    hbf[:, hc * 512:(hc + 1) * 512],
                                    start=(hc == 0), stop=(hc == HC - 1))
                                if hc == HC - 1:
                                    cP.inc(mm)
                                    p_w2[p2 * 8 + j] = cP.n
                        if hh == 0:
                            cP.inc(mm)
                        w2h_free[buf] = cP.n
                    for j in range(8):
                        oc2 = p2 * 8 + j
                        VE.wait(cP, p_w2[oc2])
                        VE.wait(d_acc, lda)
                        NV.scalar_tensor_tensor(
                            acc[:, oc2 * 512:(oc2 + 1) * 512],
                            pb[j][:, 0:512], 1.0,
                            acc[:, oc2 * 512:(oc2 + 1) * 512],
                            ALU.mult, ALU.add)
                        bank_free[j] = ("V", cV.n)
                hbf_free = cP.n
                v_acc_done = cV.n
                if routed and ee == 7:
                    SY.wait(cV, v_acc_done)
                    acc_free = d_yt.inc(nc.sync.dma_start(
                        yt[:, t0:t0 + 512].rearrange("(c p) t -> p c t", p=128),
                        acc[:].rearrange("p (c t) -> p c t", t=512)), 16)
                    xn2q_free = cP.n
                    ball_free = cV.n
                if pi + 1 < len(pairs):
                    w1_ld = w1_ld_next
                    w3_ld = w3_ld_next

            SY.wait(d_yt, d_yt.n)
    return nc


def _host_prep(inputs):
    x = np.asarray(inputs["L0_attn_x"], np.float32)
    wq = np.ascontiguousarray(np.asarray(inputs["L0_attn_wq"], np.float32))
    wk = np.ascontiguousarray(np.asarray(inputs["L0_attn_wk"], np.float32))
    wv = np.ascontiguousarray(np.asarray(inputs["L0_attn_wv"], np.float32))
    wo = np.ascontiguousarray(np.asarray(inputs["L0_attn_wo"], np.float32))
    bq = np.asarray(inputs["L0_attn_bq"], np.float32)
    bk = np.asarray(inputs["L0_attn_bk"], np.float32)
    bv = np.asarray(inputs["L0_attn_bv"], np.float32)
    bo = np.asarray(inputs["L0_attn_bo"], np.float32)
    gate_w = np.asarray(inputs["L0_ffn_gate_w"], np.float32)
    w1 = np.asarray(inputs["L0_ffn_w1"], np.float32)
    w2 = np.asarray(inputs["L0_ffn_w2"], np.float32)
    w3 = np.asarray(inputs["L0_ffn_w3"], np.float32)
    sw1 = np.asarray(inputs["L0_ffn_sw1"], np.float32)
    sw2 = np.asarray(inputs["L0_ffn_sw2"], np.float32)
    sw3 = np.asarray(inputs["L0_ffn_sw3"], np.float32)

    bq2 = np.ascontiguousarray(bq.reshape(H, 128).T)
    bk2 = np.ascontiguousarray(bk.reshape(G, 128).T)
    bv2 = np.ascontiguousarray(bv.reshape(G, 128).T)
    bo2 = np.ascontiguousarray(bo.reshape(DC, 128).T)

    inv_freq = (1.0 / (np.float32(10000.0) **
                       (np.arange(0, HD, 2, dtype=np.float32) / np.float32(HD))))
    t = np.arange(T, dtype=np.float32)
    freqs = np.concatenate([np.outer(t, inv_freq)] * 2, axis=-1).astype(np.float32)
    sin = np.sin(freqs).astype(np.float32)
    cos = np.cos(freqs).astype(np.float32)
    sin_signed = sin.copy()
    sin_signed[:, : HD // 2] *= -1.0
    cosT = np.ascontiguousarray(cos.T)
    sinT = np.ascontiguousarray(sin_signed.T)
    ropet = np.concatenate(
        [cosT * np.float32(1.0 / 128.0), sinT * np.float32(1.0 / 128.0),
         cosT, sinT], axis=1).astype(np.float32)

    identf = np.eye(128, dtype=np.float32)
    oneh = np.zeros((8, 8 * 128), np.float32)
    for ee in range(8):
        oneh[ee, ee * 128:(ee + 1) * 128] = 1.0
    gwt = np.ascontiguousarray(gate_w.T)

    w1t = np.concatenate([w1.transpose(0, 2, 1),
                          sw1.T[None]], axis=0)
    w3t = np.concatenate([w3.transpose(0, 2, 1),
                          sw3.T[None]], axis=0)
    w2t = np.concatenate([w2.transpose(0, 2, 1),
                          sw2.T[None]], axis=0)
    w1t = np.ascontiguousarray(w1t, np.float32)
    w3t = np.ascontiguousarray(w3t, np.float32)
    w2t = np.ascontiguousarray(w2t, np.float32)

    shared = {
        "wq": wq, "wk": wk, "wv": wv, "wo": wo,
        "bq2": bq2, "bk2": bk2, "bv2": bv2, "bo2": bo2,
        "ropet": ropet, "identf": identf, "oneh": oneh, "gwt": gwt,
        "w1t": w1t, "w3t": w3t, "w2t": w2t,
    }
    in_maps = []
    for b in range(8):
        m = dict(shared)
        m["xt"] = np.ascontiguousarray(x[b].T)
        in_maps.append(m)
    return in_maps


def kernel(**inputs):
    if "nc" not in _CACHE:
        _CACHE["nc"] = build_program()
    nc = _CACHE["nc"]
    in_maps = _host_prep(inputs)
    res = run_bass_kernel_spmd(nc, in_maps, core_ids=list(range(8)))
    out = np.stack([np.asarray(r["yt"], np.float32).T for r in res.results], axis=0)
    return out
